# version 15
# baseline (speedup 1.0000x reference)
import numpy as np
from contextlib import ExitStack

import concourse.bass as bass
import concourse.mybir as mybir
from concourse.bass_utils import run_bass_kernel_spmd
import ml_dtypes

F32 = mybir.dt.float32
BF16 = mybir.dt.bfloat16
ALU = mybir.AluOpType
ACT = mybir.ActivationFunctionType
AX = mybir.AxisListType

SEG = 4096
D = 1024
NH = 4
DK = 256
DV = 512
DF = 2048
C = 256
T1 = 512
NT1 = SEG // T1
T2 = 256
NT2 = SEG // T2
LN16 = float(np.log(16.0))
EPS = 1e-5
PAIRS = [[0, 1], [2, 3], [4, 5], [6, 7]]


class Sem:
    def __init__(self, h):
        self.h = h
        self.n = 0

    def reserve(self, k=1):
        self.n += k
        return self.n

    def inc(self, instr, k=1):
        instr.then_inc(self.h, k)
        self.n += k
        return self.n


def wt(eng, sem, val=None):
    eng.wait_ge(sem.h, sem.n if val is None else val)


class TQ:
    ORDER = ["sync", "gpsimd", "tensor", "scalar", "vector"]

    def __init__(self, blk):
        self.blk = blk
        self.q = {k: [] for k in self.ORDER}

    def add(self, eng, fn):
        self.q[eng].append(fn)

    def flush(self):
        for name in self.ORDER:
            fns = self.q[name]
            if fns:
                getattr(self.blk, name)(
                    lambda eng, fns=tuple(fns): [f(eng) for f in fns])
            self.q[name] = []


def stage(tq, engname, waits, incs, body):
    for s, k in incs:
        s.reserve(k)
    waits = tuple(waits)
    incs = tuple(incs)

    def f(eng):
        for s, v in waits:
            eng.wait_ge(s.h, v)

        def att(instr):
            for s, k in incs:
                instr.then_inc(s.h, k)
            return instr
        body(eng, att)
    tq.add(engname, f)


def build_program(dev_phases=None):
    dev = dev_phases is not None
    phases = dev_phases or ("p1a", "p1b", "p2a", "p2b")

    nc = bass.Bass("TRN2", target_bir_lowering=False)

    def dram_in(name, shape, dt=F32):
        return nc.dram_tensor(name, shape, dt, kind="ExternalInput").ap()

    def dram_out(name, shape, dt=F32):
        return nc.dram_tensor(name, shape, dt, kind="ExternalOutput").ap()

    def spill(name, shape, dt, produced):
        if not dev:
            return nc.dram_tensor(name, shape, dt).ap()
        return dram_out(name, shape, dt) if produced in phases else dram_in(name, shape, dt)

    io = {}
    io["x"] = dram_in("x", [128 + SEG, D])
    for n, s in [("wq", [D, D]), ("wk", [D, D]), ("wv", [D, DF]), ("wg", [D, DF]),
                 ("gw1", [D, 16]), ("gw2", [16, D]), ("gb2", [1, D]),
                 ("gnw", [1, DV]), ("wo", [DF, D]), ("fr", [D, D]),
                 ("fk", [D, D]), ("fv", [D, D]), ("mur", [1, D]), ("muk", [1, D])]:
        io[n] = dram_in(n, s)
    io["identb"] = dram_in("identb", [128, 128], BF16)
    io["caus"] = dram_in("caus", [128, 2 * C], BF16)
    io["role"] = dram_in("role", [128, 1])
    io["use"] = dram_in("use", [128, 1])
    io["out"] = dram_out("out", [SEG, D])

    io["y_sp"] = spill("y_sp", [SEG, D], F32, "p1a")
    io["z_sp"] = spill("z_sp", [D, SEG], BF16, "p1a")
    io["qg_sp"] = spill("qg_sp", [D, SEG], BF16, "p1b")
    io["o_sp"] = spill("o_sp", [DF, SEG], BF16, "p1b")
    io["u_sp"] = spill("u_sp", [SEG, D], F32, "p2a")
    io["bl_sp"] = spill("bl_sp", [128, 128], F32, "p1b")
    io["l3_sp"] = spill("l3_sp", [128, 64], F32, "p2a")
    io["st_snd"] = spill("st_snd", [128, 8 * DV], BF16, "p1b")
    io["ub_snd"] = spill("ub_snd", [1, D], F32, "p2a")
    if dev:
        io["st_rcv"] = dram_in("st_rcv", [128, 8 * DV], BF16)
        io["ub_rcv"] = dram_in("ub_rcv", [1, D])
    else:
        io["st_rcv"] = nc.dram_tensor("st_rcv", [128, 8 * DV], BF16).ap()
        io["ub_rcv"] = nc.dram_tensor("ub_rcv", [1, D], F32).ap()

    with ExitStack() as top:
        et = top.enter_context
        P = {}
        P["identb"] = et(nc.sbuf_tensor("p_identb", [128, 128], BF16))
        P["role"] = et(nc.sbuf_tensor("p_role", [128, 1], F32))
        P["use"] = et(nc.sbuf_tensor("p_use", [128, 1], F32))
        P["murs"] = et(nc.sbuf_tensor("p_murs", [128, 8], F32))
        P["muks"] = et(nc.sbuf_tensor("p_muks", [128, 8], F32))
        P["ones1"] = et(nc.sbuf_tensor("p_ones1", [1, 512], F32))
        P["onesb"] = et(nc.sbuf_tensor("p_onesb", [128, 1], BF16))
        P["gnwc"] = et(nc.sbuf_tensor("p_gnwc", [128, 4], F32))
        P["BL"] = et(nc.sbuf_tensor("p_BL", [128, 128], F32))
        P["l3m"] = et(nc.sbuf_tensor("p_l3m", [128, 32], F32))
        P["l3i"] = et(nc.sbuf_tensor("p_l3i", [128, 32], F32))
        cvals = et(nc.sbuf_tensor("p_cvals", [128, 3], F32))
        P["ceps"] = cvals[:, 0:1]
        P["cone"] = cvals[:, 1:2]
        P["cln16n"] = cvals[:, 2:3]

        S = {k: Sem(et(nc.semaphore("s_" + k)))
             for k in ["LD", "LA", "LB", "TE", "VE", "SE", "GP", "GC", "CC", "SP", "ZP", "WB",
                       "W1", "W2", "W3"]}

        with nc.Block() as blk:
            @blk.sync
            def _(sync):
                for dst, src in [(P["identb"], io["identb"]),
                                 (P["role"], io["role"]), (P["use"], io["use"])]:
                    S["LD"].inc(sync.dma_start(out=dst[:, :], in_=src[:, :]), 16)
                for dst, src in [(P["murs"], io["mur"]), (P["muks"], io["muk"])]:
                    S["LD"].inc(sync.dma_start(
                        out=dst[:, :], in_=src[0, :].rearrange("(c p) -> p c", p=128),
                        allow_slow_non_contiguous=True), 16)
                S["LD"].inc(sync.dma_start(
                    out=P["gnwc"][:, :], in_=io["gnw"][0, :].rearrange("(c p) -> p c", p=128),
                    allow_slow_non_contiguous=True), 16)

            @blk.vector
            def _(vector):
                wt(vector, S["LD"])
                vector.memset(P["ones1"][:, :], 1.0)
                vector.memset(P["onesb"][:, :], 1.0)
                vector.memset(P["BL"][:, :], 0.0)
                vector.memset(P["l3m"][:, :], 0.0)
                vector.memset(P["l3i"][:, :], 0.0)
                vector.memset(P["ceps"], EPS)
                vector.memset(P["cone"], 1.0)
                vector.memset(P["cln16n"], -LN16)

        if "p1a" in phases:
            phase_ffn(nc, io, P, S, which="1a")
        if "p1b" in phases:
            phase_p1b(nc, io, P, S)
        if "p2a" in phases:
            phase_p2a(nc, io, P, S, dev)
        if "p2b" in phases:
            phase_ffn(nc, io, P, S, which="2b", dev=dev)

    return nc


def phase_ffn(nc, io, P, S, which, dev=False):
    p2b = which == "2b"
    tag = which
    LD, TE, VE, SE, GP = (S[k] for k in ["LD", "TE", "VE", "SE", "GP"])
    GC, ZP = S["GC"], S["ZP"]
    LA, LB = S["LA"], S["LB"]

    with ExitStack() as ctx:
        e = ctx.enter_context
        ffnT = {w: e(nc.sbuf_tensor(f"ffnT_{w}_{tag}", [128, 8 * D], BF16))
                for w in ["fr", "fk", "fv"]}
        pstA = e(nc.psum_tensor(f"pstA_{tag}", [128, 1024], BF16))
        pstB = e(nc.psum_tensor(f"pstB_{tag}", [128, 1024], BF16))
        guards = {}

        def gw(name):
            g = guards.get(name)
            return [g] if g is not None else []

        def gset(name, sem):
            guards[name] = (sem, sem.n)

        with ExitStack() as pctx:
            wstage = [pctx.enter_context(
                nc.sbuf_tensor(f"wstage{i}_{tag}", [128, 8 * D], BF16)) for i in range(2)]
            WS = [S["W1"], S["W2"], S["W3"]]
            with nc.Block() as blk:
                tq = TQ(blk)
                if p2b:
                    stage(tq, "sync", [(GP, GP.n)], [(LD, 16)], lambda s, att: att(
                        s.dma_start(out=P["l3m"][:, :], in_=io["l3_sp"][:, 0:32])))
                    stage(tq, "sync", [], [(LD, 16)], lambda s, att: att(
                        s.dma_start(out=P["l3i"][:, :], in_=io["l3_sp"][:, 32:64])))
                tr_done = [None, None, None]
                for wi, w in enumerate(["fr", "fk", "fv"]):
                    ws = wstage[wi % 2]
                    sem = WS[wi]
                    dw = [(TE, tr_done[wi - 2])] if wi >= 2 else []
                    stage(tq, "gpsimd", dw, [(sem, 16)],
                          lambda g, att, w=w, ws=ws: att(g.dma_start(
                              out=ws[:, :].rearrange("p (c n) -> p c n", c=8),
                              in_=io[w][:, :].rearrange("(c p) n -> p c n", p=128))))
                    w_ld = sem.n
                    for dc in range(8):
                        ps = pstA if dc % 2 == 0 else pstB
                        pname = "pstA" if dc % 2 == 0 else "pstB"

                        def trbody(tensor, att, w=w, dc=dc, ps=ps, ws=ws):
                            for cc in range(8):
                                instr = tensor.transpose(
                                    ps[:, cc * 128:(cc + 1) * 128],
                                    ws[:, dc * 1024 + cc * 128:dc * 1024 + (cc + 1) * 128],
                                    P["identb"][:, :])
                            att(instr)
                        stage(tq, "tensor",
                              ([(sem, w_ld)] if dc == 0 else []) + gw(pname),
                              [(TE, 1)], trbody)

                        def evbody(scalar, att, w=w, dc=dc, ps=ps):
                            scale = 0.25 if w == "fv" else 1.0
                            out3 = ffnT[w][:, :].rearrange("p (c n) -> p c n", c=8)[:, :, dc * 128:(dc + 1) * 128]
                            att(scalar.activation(out3, ps[:, :].rearrange("p (c n) -> p c n", c=8),
                                                  ACT.Copy, scale=scale))
                        stage(tq, "scalar", [(TE, TE.n)], [(SE, 1)], evbody)
                        gset(pname, SE)
                    tr_done[wi] = TE.n
                tq.flush()

        pk = e(nc.psum_tensor(f"pk_{tag}", [128, 2048], F32))
        if not p2b:
            pzA = e(nc.psum_tensor(f"pzA_{tag}", [128, 1024], BF16))
            pzB = e(nc.psum_tensor(f"pzB_{tag}", [128, 1024], BF16))
        xt = [e(nc.sbuf_tensor(f"xt{i}_{tag}", [128, T1 * 8], F32)) for i in range(2)]
        scrap = e(nc.sbuf_tensor(f"scrap_{tag}", [128, 1024], F32))
        st1 = e(nc.sbuf_tensor(f"st1_{tag}", [128, 16], F32))
        st2 = e(nc.sbuf_tensor(f"st2_{tag}", [128, 16], F32))
        ln1tm = [e(nc.sbuf_tensor(f"ln1tm{i}_{tag}", [128, 4 * 1024], BF16)) for i in range(2)]
        lnfm = [e(nc.sbuf_tensor(f"lnfm{i}_{tag}", [128, 8 * 513], BF16)) for i in range(2)]
        xr = [e(nc.sbuf_tensor(f"xr{i}_{tag}", [128, 8 * 512], BF16)) for i in range(2)]
        xk = [e(nc.sbuf_tensor(f"xk{i}_{tag}", [128, 8 * 512], BF16)) for i in range(2)]
        kact = e(nc.sbuf_tensor(f"kact_{tag}", [128, 8 * 512], BF16))
        rth = e(nc.sbuf_tensor(f"rth_{tag}", [128, 4 * 1024], BF16))
        yt = e(nc.sbuf_tensor(f"yt_{tag}", [128, 4 * 1024], F32))
        bx = yt
        if not p2b:
            ztm = e(nc.sbuf_tensor(f"ztm_{tag}", [128, 4 * 1024], BF16))
            zfm = e(nc.sbuf_tensor(f"zfm_{tag}", [128, 8 * 512], BF16))
        lnfm3s = [lnfm[i][:, :].rearrange("p (c n) -> p c n", n=513) for i in range(2)]

        with nc.Block() as blk:
            tq = TQ(blk)
            xt_ready = [None] * NT1

            @blk.sync
            def _(sync):
                if not p2b:
                    LD.inc(sync.dma_start(out=bx[:, 0:1024], in_=io["x"][0:128, :]), 16)
                src = io["u_sp"] if p2b else io["x"]
                off = 0 if p2b else 128
                xt_ready[0] = (LA, LA.inc(sync.dma_start(
                    out=xt[0][:, :].rearrange("p (g n) -> p g n", g=4),
                    in_=src[off:off + T1, :].rearrange("(g p) n -> p g n", p=128)), 16))

            if p2b:
                @blk.vector
                def _(vector):
                    VE.inc(vector.memset(bx[:, 0:1024], 0.0))

                @blk.gpsimd
                def _(gpsimd):
                    wt(gpsimd, VE)
                    if not dev:
                        wt(gpsimd, S["CC"])
                    GP.inc(gpsimd.dma_start(out=bx[127:128, 0:1024], in_=io["ub_rcv"][:, :]), 16)

            @blk.vector
            def _(vector):
                if p2b:
                    wt(vector, GP)
                else:
                    wt(vector, LD)
                vector.reduce_sum(st1[:, 0:1], bx[:, 0:1024], axis=AX.X)
                vector.drain()
                VE.inc(vector.tensor_scalar(st1[:, 0:1], st1[:, 0:1], 1.0 / D, None,
                                            ALU.mult, ALU.bypass))

            @blk.scalar
            def _(scalar):
                wt(scalar, VE)
                SE.inc(scalar.activation(scrap[:, :], bx[:, 0:1024], ACT.Square,
                                         accum_out=st1[:, 4:5]))

            @blk.vector
            def _(vector):
                wt(vector, SE)
                vector.drain()
                vector.tensor_tensor(st1[:, 8:9], st1[:, 0:1], st1[:, 0:1], ALU.mult)
                vector.drain()
                VE.inc(vector.scalar_tensor_tensor(
                    st1[:, 8:9], st1[:, 4:5], 1.0 / D, st1[:, 8:9],
                    ALU.mult, ALU.subtract))

            @blk.scalar
            def _(scalar):
                wt(scalar, VE)
                SE.inc(scalar.activation(st1[:, 12:13], st1[:, 8:9], ACT.Sqrt, bias=P["ceps"]))

            @blk.vector
            def _(vector):
                wt(vector, SE)
                vector.drain()
                vector.reciprocal(st1[:, 12:13], st1[:, 12:13])
                vector.drain()
                VE.inc(vector.tensor_scalar(ln1tm[0][:, 0:1024], bx[:, 0:1024],
                                            st1[:, 0:1], st1[:, 12:13],
                                            ALU.subtract, ALU.mult))

            @blk.tensor
            def _(tensor):
                wt(tensor, VE)
                for g_ in gw("pstA"):
                    tensor.wait_ge(g_[0].h, g_[1])
                for cc in range(8):
                    instr = tensor.transpose(pstA[:, cc * 128:(cc + 1) * 128],
                                             ln1tm[0][:, cc * 128:(cc + 1) * 128],
                                             P["identb"][:, :])
                TE.inc(instr)

            @blk.vector
            def _(vector):
                wt(vector, TE)
                src = pstA[:, :].rearrange("p (c n) -> p c n", c=8)[:, :, 127:128]
                dst = lnfm3s[0][:, :, 0:1]
                vector.drain()
                vector.tensor_copy(dst, src)
                vector.drain()
                VE.inc(vector.tensor_scalar(dst, dst, P["use"][:, :], None,
                                            ALU.mult, ALU.bypass))
            gset("pstA", VE)

            def ln1_chain(tt):
                pp = tt % 2
                if not p2b:
                    def v_stats(vector, att, pp=pp):
                        vector.drain()
                        for g in range(4):
                            vector.reduce_sum(st1[:, g:g + 1],
                                              xt[pp][:, g * 1024:(g + 1) * 1024], axis=AX.X)
                        vector.drain()
                        att(vector.tensor_scalar(st1[:, 0:4], st1[:, 0:4], 1.0 / D,
                                                 None, ALU.mult, ALU.bypass))
                    stage(tq, "vector", [xt_ready[tt]], [(VE, 1)], v_stats)

                    def s_sq(scalar, att, pp=pp):
                        for g in range(4):
                            scalar.drain()
                            instr = scalar.activation(scrap[:, :],
                                                      xt[pp][:, g * 1024:(g + 1) * 1024],
                                                      ACT.Square, accum_out=st1[:, 4 + g:5 + g])
                        att(instr)
                    stage(tq, "scalar", [(VE, VE.n)], [(SE, 1)], s_sq)

                    def v_var(vector, att):
                        vector.drain()
                        vector.tensor_tensor(st1[:, 8:12], st1[:, 0:4], st1[:, 0:4], ALU.mult)
                        vector.drain()
                        att(vector.scalar_tensor_tensor(
                            st1[:, 8:12], st1[:, 4:8], 1.0 / D, st1[:, 8:12],
                            ALU.mult, ALU.subtract))
                    stage(tq, "vector", [(SE, SE.n)], [(VE, 1)], v_var)

                    stage(tq, "scalar", [(VE, VE.n)], [(SE, 1)],
                          lambda scalar, att: att(scalar.activation(
                              st1[:, 12:16], st1[:, 8:12], ACT.Sqrt, bias=P["ceps"])))

                    def v_apply(vector, att, pp=pp):
                        vector.drain()
                        vector.reciprocal(st1[:, 12:16], st1[:, 12:16])
                        vector.drain()
                        for g in range(4):
                            instr = vector.tensor_scalar(
                                ln1tm[pp][:, g * 1024:(g + 1) * 1024],
                                xt[pp][:, g * 1024:(g + 1) * 1024],
                                st1[:, g:g + 1], st1[:, 12 + g:13 + g],
                                ALU.subtract, ALU.mult)
                        att(instr)
                    stage(tq, "vector", [(SE, SE.n), (TE, TE.n)], [(VE, 1)], v_apply)
                    ln1_done[tt] = VE.n
                else:
                    def v_apply(vector, att, tt=tt, pp=pp):
                        vector.drain()
                        for g in range(4):
                            gg = tt * 4 + g
                            instr = vector.tensor_scalar(
                                ln1tm[pp][:, g * 1024:(g + 1) * 1024],
                                xt[pp][:, g * 1024:(g + 1) * 1024],
                                P["l3m"][:, gg:gg + 1], P["l3i"][:, gg:gg + 1],
                                ALU.subtract, ALU.mult)
                        att(instr)
                    stage(tq, "vector", [xt_ready[tt], (TE, TE.n), (LD, LD.n)],
                          [(VE, 1)], v_apply)
                    ln1_done[tt] = VE.n

            ln1_done = [None] * NT1
            mix_done_arr = [None] * NT1
            mixk_done_arr = [None] * NT1
            yadd_done_arr = [None] * NT1
            ybody_last = [None] * NT1
            v2a_done = [None] * NT1
            yspill_done = [None] * NT1
            zspill_prev = [ZP.n]

            def mix_group(tt):
                pp = tt % 2
                for g in range(4):
                    ps = pstA if g % 2 == 0 else pstB
                    pname = "pstA" if g % 2 == 0 else "pstB"

                    def trbody(tensor, att, g=g, ps=ps, pp=pp):
                        for cc in range(8):
                            instr = tensor.transpose(
                                ps[:, cc * 128:(cc + 1) * 128],
                                ln1tm[pp][:, g * 1024 + cc * 128:g * 1024 + (cc + 1) * 128],
                                P["identb"][:, :])
                        att(instr)
                    stage(tq, "tensor", [(VE, ln1_done[tt])] + gw(pname), [(TE, 1)], trbody)

                    def evbody(scalar, att, g=g, ps=ps, pp=pp):
                        scalar.drain()
                        dst = lnfm3s[pp][:, :, 1 + g * 128:1 + (g + 1) * 128]
                        att(scalar.activation(dst, ps[:, :].rearrange("p (c n) -> p c n", c=8),
                                              ACT.Copy))
                    evw = [(TE, TE.n)]
                    if g == 0 and tt >= 2:
                        evw.append((VE, mix_done_arr[tt - 2]))
                    stage(tq, "scalar", evw, [(SE, 1)], evbody)
                    gset(pname, SE)

                def v_mix_k(vector, att, pp=pp):
                    xr3 = xr[pp][:, :].rearrange("p (c n) -> p c n", n=512)
                    xk3 = xk[pp][:, :].rearrange("p (c n) -> p c n", n=512)
                    lf = lnfm3s[pp]
                    vector.drain()
                    vector.tensor_tensor(xr3, lf[:, :, 1:513], lf[:, :, 0:512], ALU.subtract)
                    vector.drain()
                    for cc in range(8):
                        instr = vector.scalar_tensor_tensor(
                            xk3[:, cc, :], xr3[:, cc, :], P["muks"][:, cc:cc + 1],
                            lf[:, cc, 0:512], ALU.mult, ALU.add)
                    att(instr)
                stage(tq, "vector", [(TE, TE.n), (SE, SE.n)], [(VE, 1)], v_mix_k)
                mixk_done_arr[tt] = VE.n

            def mix_group_b(tt):
                pp = tt % 2

                def v_mix_r(vector, att, pp=pp):
                    xr3 = xr[pp][:, :].rearrange("p (c n) -> p c n", n=512)
                    lf = lnfm3s[pp]
                    vector.drain()
                    for cc in range(8):
                        vector.scalar_tensor_tensor(
                            xr3[:, cc, :], xr3[:, cc, :], P["murs"][:, cc:cc + 1],
                            lf[:, cc, 0:512], ALU.mult, ALU.add)
                    vector.drain()
                    att(vector.tensor_copy(lnfm3s[1 - pp][:, :, 0:1], lf[:, :, 512:513]))
                stage(tq, "vector", [], [(VE, 1)], v_mix_r)
                mix_done_arr[tt] = VE.n

            def ln2_group(tt):
                def v2_stats(vector, att):
                    vector.drain()
                    for g in range(4):
                        vector.reduce_sum(st2[:, g:g + 1], yt[:, g * 1024:(g + 1) * 1024],
                                          axis=AX.X)
                    vector.drain()
                    att(vector.tensor_scalar(st2[:, 0:4], st2[:, 0:4], 1.0 / D,
                                             None, ALU.mult, ALU.bypass))
                stage(tq, "vector", [(GC, yadd_done_arr[tt])], [(VE, 1)], v2_stats)

                def s2_sq(scalar, att):
                    for g in range(4):
                        scalar.drain()
                        instr = scalar.activation(scrap[:, :], yt[:, g * 1024:(g + 1) * 1024],
                                                  ACT.Square, accum_out=st2[:, 4 + g:5 + g])
                    att(instr)
                stage(tq, "scalar", [(GC, yadd_done_arr[tt])], [(SE, 1)], s2_sq)

                def v2_var(vector, att):
                    vector.drain()
                    vector.tensor_tensor(st2[:, 8:12], st2[:, 0:4], st2[:, 0:4], ALU.mult)
                    vector.drain()
                    att(vector.scalar_tensor_tensor(
                        st2[:, 8:12], st2[:, 4:8], 1.0 / D, st2[:, 8:12],
                        ALU.mult, ALU.subtract))
                stage(tq, "vector", [(SE, SE.n)], [(VE, 1)], v2_var)

                stage(tq, "scalar", [(VE, VE.n)], [(SE, 1)],
                      lambda scalar, att: att(scalar.activation(
                          st2[:, 12:16], st2[:, 8:12], ACT.Sqrt, bias=P["ceps"])))

                def v2_apply(vector, att):
                    vector.drain()
                    vector.reciprocal(st2[:, 12:16], st2[:, 12:16])
                    vector.drain()
                    for g in range(4):
                        instr = vector.tensor_scalar(
                            ztm[:, g * 1024:(g + 1) * 1024],
                            yt[:, g * 1024:(g + 1) * 1024],
                            st2[:, g:g + 1], st2[:, 12 + g:13 + g],
                            ALU.subtract, ALU.mult)
                    att(instr)
                stage(tq, "vector", [(SE, SE.n), (TE, TE.n), (GC, yadd_done_arr[tt])],
                      [(VE, 1)], v2_apply)
                v2a_done[tt] = VE.n

            def ztr_group(tt):
                for g in range(4):
                    ps = pzA if g % 2 == 0 else pzB
                    pname = "pzA" if g % 2 == 0 else "pzB"

                    def trz(tensor, att, g=g, ps=ps):
                        for cc in range(8):
                            instr = tensor.transpose(
                                ps[:, cc * 128:(cc + 1) * 128],
                                ztm[:, g * 1024 + cc * 128:g * 1024 + (cc + 1) * 128],
                                P["identb"][:, :])
                        att(instr)
                    trw = [(VE, v2a_done[tt])] if g == 0 else []
                    stage(tq, "tensor", trw + gw(pname), [(TE, 1)], trz)

                    def evz(scalar, att, g=g, ps=ps):
                        scalar.drain()
                        dst = zfm[:, :].rearrange("p (c n) -> p c n", n=512)[:, :, g * 128:(g + 1) * 128]
                        att(scalar.activation(dst, ps[:, :].rearrange("p (c n) -> p c n", c=8),
                                              ACT.Copy))
                    evw2 = [(TE, TE.n)]
                    if g == 0:
                        evw2.append((ZP, zspill_prev[0]))
                    stage(tq, "scalar", evw2, [(SE, 1)], evz)
                    gset(pname, SE)

                stage(tq, "gpsimd", [(SE, SE.n)], [(ZP, 16)],
                      lambda gp, att, tt=tt: att(gp.dma_start(
                          out=io["z_sp"][:, tt * T1:(tt + 1) * T1].rearrange("(c p) n -> p c n", p=128),
                          in_=zfm[:, :].rearrange("p (c n) -> p c n", c=8))))
                zspill_prev[0] = ZP.n

            ln1_chain(0)
            mix_group(0)
            mix_group_b(0)

            for t in range(NT1):
                par = t % 2

                if t + 1 < NT1:
                    psem = LB if (t + 1) % 2 == 1 else LA
                    pw = [(VE, VE.n), (GC, GC.n)] if t >= 1 else []
                    src = io["u_sp"] if p2b else io["x"]
                    off = (0 if p2b else 128) + (t + 1) * T1
                    stage(tq, "sync", pw, [(psem, 16)],
                          lambda s, att, src=src, off=off, par=par: att(s.dma_start(
                              out=xt[1 - par][:, :].rearrange("p (g n) -> p g n", g=4),
                              in_=src[off:off + T1, :].rearrange("(g p) n -> p g n", p=128))))
                    xt_ready[t + 1] = (psem, psem.n)

                if (not p2b) and t > 0:
                    ln2_group(t - 1)

                for p2 in range(4):
                    half = p2 % 2
                    pname = f"pk{half}"

                    def kbody(tensor, att, p2=p2, half=half, par=par):
                        for o2 in range(2):
                            oc = p2 * 2 + o2
                            for ic in range(8):
                                instr = tensor.matmul(
                                    pk[:, half * 1024 + o2 * 512:half * 1024 + (o2 + 1) * 512],
                                    ffnT["fk"][:, ic * 1024 + oc * 128:ic * 1024 + (oc + 1) * 128],
                                    xk[par][:, ic * 512:(ic + 1) * 512],
                                    start=(ic == 0), stop=(ic == 7))
                        att(instr)
                    stage(tq, "tensor",
                          ([(VE, mixk_done_arr[t])] if p2 == 0 else []) + gw(pname),
                          [(TE, 1)], kbody)

                    def kevac(scalar, att, p2=p2, half=half):
                        sl = slice(p2 * 1024, (p2 + 1) * 1024)
                        psl = pk[:, half * 1024:(half + 1) * 1024]
                        scalar.drain()
                        scalar.activation(kact[:, sl], psl, ACT.Relu)
                        scalar.drain()
                        att(scalar.activation(kact[:, sl], kact[:, sl], ACT.Square))
                    stage(tq, "scalar", [(TE, TE.n)], [(SE, 1)], kevac)
                    gset(pname, SE)
                kact_done = SE.n

                if t + 1 < NT1:
                    ln1_chain(t + 1)

                for tg in range(4):
                    half = tg % 2
                    pname = f"pk{half}"

                    def rbody(tensor, att, tg=tg, half=half, par=par):
                        for j in range(2):
                            for ic in range(8):
                                instr = tensor.matmul(
                                    pk[:, half * 1024 + j * 512:half * 1024 + (j + 1) * 512],
                                    xr[par][:, ic * 512 + tg * 128:ic * 512 + (tg + 1) * 128],
                                    ffnT["fr"][:, ic * 1024 + j * 512:ic * 1024 + (j + 1) * 512],
                                    start=(ic == 0), stop=(ic == 7))
                        att(instr)
                    rbw = [(VE, mix_done_arr[t])] if tg == 0 else []
                    stage(tq, "tensor", rbw + gw(pname), [(TE, 1)], rbody)

                    def tanhb(scalar, att, tg=tg, half=half):
                        scalar.drain()
                        att(scalar.activation(rth[:, tg * 1024:(tg + 1) * 1024],
                                              pk[:, half * 1024:(half + 1) * 1024],
                                              ACT.Tanh, scale=0.5))
                    tanhw = [(TE, TE.n)]
                    if t > 0 and tg == 0:
                        tanhw.append((VE, ybody_last[t - 1]))
                    stage(tq, "scalar", tanhw, [(SE, 1)], tanhb)
                    gset(pname, SE)

                if t + 1 < NT1:
                    mix_group(t + 1)

                for tg in range(4):
                    half = tg % 2
                    pname = f"pk{half}"

                    def kvbody(tensor, att, tg=tg, half=half):
                        for j in range(2):
                            for ic in range(8):
                                instr = tensor.matmul(
                                    pk[:, half * 1024 + j * 512:half * 1024 + (j + 1) * 512],
                                    kact[:, ic * 512 + tg * 128:ic * 512 + (tg + 1) * 128],
                                    ffnT["fv"][:, ic * 1024 + j * 512:ic * 1024 + (j + 1) * 512],
                                    start=(ic == 0), stop=(ic == 7))
                        att(instr)
                    kvw = [(SE, kact_done)] if tg < 2 else []
                    stage(tq, "tensor", kvw + gw(pname), [(TE, 1)], kvbody)

                    def ybody(vector, att, tg=tg, half=half):
                        sl = slice(tg * 1024, (tg + 1) * 1024)
                        vector.drain()
                        att(vector.scalar_tensor_tensor(yt[:, sl], rth[:, sl], 1.0,
                                                        pk[:, half * 1024:(half + 1) * 1024],
                                                        ALU.add, ALU.mult))
                    yw = [(TE, TE.n), (SE, SE.n)]
                    if t > 0 and tg == 0:
                        yw.append((GP, yspill_done[t - 1]))
                    stage(tq, "vector", yw, [(VE, 1)], ybody)
                    gset(pname, VE)

                    def yadd(gpsimd, att, tg=tg, par=par):
                        sl = slice(tg * 1024, (tg + 1) * 1024)
                        att(gpsimd.tensor_tensor(yt[:, sl], yt[:, sl],
                                                 xt[par][:, sl], ALU.add))
                    stage(tq, "gpsimd", [(VE, VE.n), (GC, GC.n)], [(GC, 1)], yadd)
                ybody_last[t] = VE.n
                yadd_done_arr[t] = GC.n
                if t + 1 < NT1:
                    mix_group_b(t + 1)

                if p2b:
                    stage(tq, "gpsimd", [(GC, yadd_done_arr[t])], [(GP, 16)],
                          lambda g, att, t=t: att(g.dma_start(
                              out=io["out"][t * T1:(t + 1) * T1, :].rearrange("(g p) n -> p g n", p=128),
                              in_=yt[:, :].rearrange("p (g n) -> p g n", g=4))))
                else:
                    stage(tq, "gpsimd", [(GC, yadd_done_arr[t])], [(GP, 16)],
                          lambda g, att, t=t: att(g.dma_start(
                              out=io["y_sp"][t * T1:(t + 1) * T1, :].rearrange("(g p) n -> p g n", p=128),
                              in_=yt[:, :].rearrange("p (g n) -> p g n", g=4))))
                yspill_done[t] = GP.n

                if (not p2b) and t > 0:
                    ztr_group(t - 1)
                tq.flush()

            if not p2b:
                ln2_group(NT1 - 1)
                ztr_group(NT1 - 1)
                tq.flush()

            @blk.gpsimd
            def _(gpsimd):
                wt(gpsimd, GP)
                if not p2b:
                    wt(gpsimd, ZP)


def phase_p1b(nc, io, P, S):
    LD, TE, VE, SE, GP = (S[k] for k in ["LD", "TE", "VE", "SE", "GP"])
    LA, LB = S["LA"], S["LB"]

    with ExitStack() as ctx:
        e = ctx.enter_context
        wqb = e(nc.sbuf_tensor("wqb", [128, 8 * D], BF16))
        wkb = e(nc.sbuf_tensor("wkb", [128, 8 * D], BF16))
        wvb = e(nc.sbuf_tensor("wvb", [128, 8 * DF], BF16))
        gw1b = e(nc.sbuf_tensor("gw1b", [128, 8 * 16], BF16))
        gw2b = e(nc.sbuf_tensor("gw2b", [16, D], BF16))
        Hf = e(nc.sbuf_tensor("Hf", [128, 8 * DV], F32))
        Hbf = e(nc.sbuf_tensor("Hbf", [128, 8 * DV], BF16))
        zt = [e(nc.sbuf_tensor(f"zt{i}", [128, 8 * T2], BF16)) for i in range(2)]
        gk1sb = e(nc.sbuf_tensor("gk1sb", [16, T2], BF16))
        nbc = [e(nc.sbuf_tensor(f"nbc{i}", [128, 8 * T2], F32)) for i in range(2)]
        e1b = [e(nc.sbuf_tensor(f"e1b{i}", [128, 8 * T2], F32)) for i in range(2)]
        expb = e(nc.sbuf_tensor("expb", [128, 8 * T2], F32))
        expnb = e(nc.sbuf_tensor("expnb", [128, 8 * T2], F32))
        ebl8 = [e(nc.sbuf_tensor(f"ebl8_{i}", [128, 8], F32)) for i in range(2)]
        qg = e(nc.sbuf_tensor("qg", [128, 8 * T2], BF16))
        kg = e(nc.sbuf_tensor("kg", [128, 8 * T2], BF16))
        kH = e(nc.sbuf_tensor("kH", [128, 8 * T2], BF16))
        kHtm = e(nc.sbuf_tensor("kHtm", [128, 2 * D], BF16))
        vtm = e(nc.sbuf_tensor("vtm", [128, 2 * DF], BF16))
        Asb = e(nc.sbuf_tensor("Asb", [128, 8 * T2], BF16))
        osb = e(nc.sbuf_tensor("osb", [128, 16 * T2], BF16))
        caus2 = e(nc.sbuf_tensor("caus2_1b", [128, 4 * C], BF16))
        b2row = e(nc.sbuf_tensor("b2row_1b", [1, D], F32))
        pv = e(nc.psum_tensor("pv_1b", [128, 2048], F32))
        po = e(nc.psum_tensor("po_1b", [128, 1024], F32))
        pb = e(nc.psum_tensor("pb_1b", [128, 1024], BF16))
        pg = e(nc.psum_tensor("pg_1b", [128, 512], F32))

        guards = {}

        def gw(name):
            g = guards.get(name)
            return [g] if g is not None else []

        def gset(name, sem):
            guards[name] = (sem, sem.n)

        with nc.Block() as blk:
            tq = TQ(blk)

            W1, W2, W3 = S["W1"], S["W2"], S["W3"]

            @blk.gpsimd
            def _(gpsimd):
                GP.inc(gpsimd.dma_start(
                    out=gw1b[:, :].rearrange("p (c n) -> p c n", c=8),
                    in_=io["gw1"][:, :].rearrange("(c p) n -> p c n", p=128)), 16)
                GP.inc(gpsimd.dma_start(out=gw2b[:, :], in_=io["gw2"][:, :]), 16)
                GP.inc(gpsimd.dma_start(out=b2row[:, :], in_=io["gb2"][:, :]), 16)
                GP.inc(gpsimd.dma_start(out=caus2[:, 0:2 * C], in_=io["caus"][:, :]), 16)
                GP.inc(gpsimd.dma_start(out=caus2[:, 2 * C:4 * C], in_=io["caus"][:, :]), 16)
                for w, dst, sem in [("wq", wqb, W1), ("wk", wkb, W2), ("wv", wvb, W3)]:
                    sem.inc(gpsimd.dma_start(
                        out=dst[:, :].rearrange("p (c n) -> p c n", c=8),
                        in_=io[w][:, :].rearrange("(c p) n -> p c n", p=128)), 16)
            wq_ld, wk_ld, wv_ld = W1.n, W2.n, W3.n

            @blk.vector
            def _(vector):
                vector.memset(Hf[:, :], 0.0)
                VE.inc(vector.memset(Hbf[:, :], 0.0))

            zt_ready = [None] * NT2

            @blk.sync
            def _(sync):
                zt_ready[0] = (LA, LA.inc(sync.dma_start(
                    out=zt[0][:, :].rearrange("p (c n) -> p c n", c=8),
                    in_=io["z_sp"][:, 0:T2].rearrange("(c p) n -> p c n", p=128)), 16))

            spill_done = GP.n
            w_ready = GP.n

            def gk_chain(tt):
                pp = tt % 2

                def gk1b(tensor, att, pp=pp):
                    for ic in range(8):
                        instr = tensor.matmul(
                            pg[0:16, 0:T2], gw1b[:, ic * 16:(ic + 1) * 16],
                            zt[pp][:, ic * T2:(ic + 1) * T2],
                            start=(ic == 0), stop=(ic == 7))
                    att(instr)
                gkw = [zt_ready[tt]] + gw("pg")
                if tt == 0:
                    gkw.append((GP, w_ready))
                stage(tq, "tensor", gkw, [(TE, 1)], gk1b)

                def gk1ev(vector, att):
                    vector.drain()
                    att(vector.tensor_copy(gk1sb[:, :], pg[0:16, 0:T2]))
                stage(tq, "vector", [(TE, TE.n)], [(VE, 1)], gk1ev)
                gset("pg", VE)

                for rr in range(4):
                    def gk2b(tensor, att, rr=rr):
                        for o2 in range(2):
                            oc = rr * 2 + o2
                            tensor.matmul(
                                pg[:, o2 * T2:(o2 + 1) * T2],
                                gw2b[:, oc * 128:(oc + 1) * 128], gk1sb[:, :],
                                start=True, stop=False)
                            instr = tensor.matmul(
                                pg[:, o2 * T2:(o2 + 1) * T2],
                                b2row[0:1, oc * 128:(oc + 1) * 128],
                                P["ones1"][0:1, 0:T2],
                                start=False, stop=True)
                        att(instr)
                    stage(tq, "tensor", ([(VE, VE.n)] if rr == 0 else []) + gw("pg"),
                          [(TE, 1)], gk2b)

                    def gkev(scalar, att, rr=rr, pp=pp):
                        sl = slice(rr * 512, (rr + 1) * 512)
                        scalar.drain()
                        scalar.activation(e1b[pp][:, sl], pg[:, :], ACT.Exp, scale=-1.0)
                        scalar.drain()
                        att(scalar.activation(e1b[pp][:, sl], e1b[pp][:, sl],
                                              ACT.Ln, bias=P["cone"]))
                    stage(tq, "scalar", [(TE, TE.n), (VE, VE.n)], [(SE, 1)], gkev)
                    gset("pg", SE)

                    def scn(vector, att, rr=rr, pp=pp, tt=tt):
                        vector.drain()
                        for o2 in range(2):
                            oc = rr * 2 + o2
                            instr = vector.tensor_tensor_scan(
                                nbc[pp][:, oc * T2:(oc + 1) * T2],
                                e1b[pp][:, oc * T2:(oc + 1) * T2],
                                e1b[pp][:, oc * T2:(oc + 1) * T2],
                                0.0, ALU.add, ALU.bypass)
                        if rr == 3:
                            vector.drain()
                            bl3 = P["BL"][:, :].rearrange("p (c n) -> p c n", n=16)
                            nbc3 = nbc[pp][:, :].rearrange("p (c n) -> p c n", n=T2)
                            instr = vector.tensor_copy(bl3[:, :, tt:tt + 1],
                                                       nbc3[:, :, T2 - 1:T2])
                        att(instr)
                    stage(tq, "vector", [(SE, SE.n)], [(VE, 1)], scn)

            def exps_stage(tt):
                pp = tt % 2

                def exps(scalar, att, tt=tt, pp=pp):
                    bl3 = P["BL"][:, :].rearrange("p (c n) -> p c n", n=16)
                    scalar.drain()
                    scalar.activation(ebl8[pp][:, :].rearrange("p (c n) -> p c n", n=1),
                                      bl3[:, :, tt:tt + 1], ACT.Exp, scale=-1.0 / 16.0)
                    scalar.activation(expb[:, :], nbc[pp][:, :], ACT.Exp,
                                      scale=-1.0 / 16.0, bias=P["cln16n"])
                    att(scalar.activation(expnb[:, :], nbc[pp][:, :], ACT.Exp,
                                          scale=1.0 / 16.0))
                stage(tq, "scalar", [(VE, VE.n)], [(SE, 1)], exps)

            gk_chain(0)
            exps_stage(0)

            for t in range(NT2):
                par = t % 2

                if t + 1 < NT2:
                    psem = LB if (t + 1) % 2 == 1 else LA
                    pw = [(TE, TE.n)] if t >= 1 else []
                    stage(tq, "sync", pw, [(psem, 16)],
                          lambda s, att, t=t, par=par: att(s.dma_start(
                              out=zt[1 - par][:, :].rearrange("p (c n) -> p c n", c=8),
                              in_=io["z_sp"][:, (t + 1) * T2:(t + 2) * T2].rearrange("(c p) n -> p c n", p=128))))
                    zt_ready[t + 1] = (psem, psem.n)

                for p2 in range(4):
                    def qb(tensor, att, p2=p2, par=par):
                        for o2 in range(2):
                            oc = p2 * 2 + o2
                            for ic in range(8):
                                instr = tensor.matmul(
                                    pv[:, oc * T2:(oc + 1) * T2],
                                    wqb[:, ic * 1024 + oc * 128:ic * 1024 + (oc + 1) * 128],
                                    zt[par][:, ic * T2:(ic + 1) * T2],
                                    start=(ic == 0), stop=(ic == 7))
                        att(instr)
                    qbw = [(W1, wq_ld)] if t == 0 and p2 == 0 else []
                    stage(tq, "tensor", qbw + gw(f"pv{p2}"), [(TE, 1)], qb)

                    def qev(vector, att, p2=p2, t=t):
                        vector.drain()
                        att(vector.tensor_tensor(qg[:, p2 * 512:(p2 + 1) * 512],
                                                 pv[:, p2 * 512:(p2 + 1) * 512],
                                                 expb[:, p2 * 512:(p2 + 1) * 512],
                                                 ALU.mult))
                    qw = [(TE, TE.n), (SE, SE.n)]
                    if t > 0 and p2 == 0:
                        qw.append((GP, spill_done))
                    stage(tq, "vector", qw, [(VE, 1)], qev)
                    gset(f"pv{p2}", VE)

                stage(tq, "gpsimd", [(VE, VE.n)], [(GP, 16)],
                      lambda g, att, t=t: att(g.dma_start(
                          out=io["qg_sp"][:, t * T2:(t + 1) * T2].rearrange("(c p) n -> p c n", p=128),
                          in_=qg[:, :].rearrange("p (c n) -> p c n", c=8))))

                for p2 in range(4):
                    def kb(tensor, att, p2=p2, par=par):
                        for o2 in range(2):
                            oc = p2 * 2 + o2
                            for ic in range(8):
                                instr = tensor.matmul(
                                    pv[:, oc * T2:(oc + 1) * T2],
                                    wkb[:, ic * 1024 + oc * 128:ic * 1024 + (oc + 1) * 128],
                                    zt[par][:, ic * T2:(ic + 1) * T2],
                                    start=(ic == 0), stop=(ic == 7))
                        att(instr)
                    kbw = [(W2, wk_ld)] if t == 0 and p2 == 0 else []
                    stage(tq, "tensor", kbw + gw(f"pv{p2}"), [(TE, 1)], kb)

                    def kev(vector, att, p2=p2, par=par):
                        sl = slice(p2 * 512, (p2 + 1) * 512)
                        vector.drain()
                        vector.tensor_tensor(kg[:, sl], pv[:, sl], expnb[:, sl], ALU.mult)
                        vector.drain()
                        for o2 in range(2):
                            oc = p2 * 2 + o2
                            instr = vector.tensor_scalar(kH[:, oc * T2:(oc + 1) * T2],
                                                         kg[:, oc * T2:(oc + 1) * T2],
                                                         ebl8[par][:, oc:oc + 1], None,
                                                         ALU.mult, ALU.bypass)
                        att(instr)
                    stage(tq, "vector", [(TE, TE.n)], [(VE, 1)], kev)
                    gset(f"pv{p2}", VE)
                kg_done = VE.n

                for tg in range(2):
                    def ktr(tensor, att, tg=tg):
                        for dc in range(8):
                            instr = tensor.transpose(
                                pb[:, dc * 128:(dc + 1) * 128],
                                kH[:, dc * T2 + tg * 128:dc * T2 + (tg + 1) * 128],
                                P["identb"][:, :])
                        att(instr)
                    kw = [(VE, kg_done)] if tg == 0 else []
                    stage(tq, "tensor", kw + gw("pb"), [(TE, 1)], ktr)

                    def ktrev(vector, att, tg=tg):
                        vector.drain()
                        att(vector.tensor_copy(kHtm[:, tg * 1024:(tg + 1) * 1024], pb[:, :]))
                    stage(tq, "vector", [(TE, TE.n)], [(VE, 1)], ktrev)
                    gset("pb", VE)

                if t + 1 < NT2:
                    gk_chain(t + 1)

                for tg in range(2):
                    for j2 in range(2):
                        def vb(tensor, att, tg=tg, j2=j2, par=par):
                            for j in range(j2 * 2, j2 * 2 + 2):
                                for ic in range(8):
                                    instr = tensor.matmul(
                                        pv[:, j * 512:(j + 1) * 512],
                                        zt[par][:, ic * T2 + tg * 128:ic * T2 + (tg + 1) * 128],
                                        wvb[:, ic * 2048 + j * 512:ic * 2048 + (j + 1) * 512],
                                        start=(ic == 0), stop=(ic == 7))
                            att(instr)
                        vbw = [(W3, wv_ld)] if t == 0 and tg == 0 and j2 == 0 else []
                        stage(tq, "tensor", vbw + gw(f"pv{j2 * 2}") + gw(f"pv{j2 * 2 + 1}"),
                              [(TE, 1)], vb)

                        def vev(vector, att, tg=tg, j2=j2):
                            vector.drain()
                            att(vector.tensor_copy(
                                vtm[:, tg * 2048 + j2 * 1024:tg * 2048 + (j2 + 1) * 1024],
                                pv[:, j2 * 1024:(j2 + 1) * 1024]))
                        stage(tq, "vector", [(TE, TE.n)], [(VE, 1)], vev)
                        gset(f"pv{j2 * 2}", VE)
                        gset(f"pv{j2 * 2 + 1}", VE)
                vtm_done = VE.n

                for hp in range(2):
                    def ab(tensor, att, hp=hp):
                        for hh in range(2):
                            h = hp * 2 + hh
                            for sg in range(2):
                                for dc2 in range(2):
                                    oc = 2 * h + dc2
                                    instr = tensor.matmul(
                                        po[:, (hh * 2 + sg) * T2:(hh * 2 + sg + 1) * T2],
                                        kg[:, oc * T2 + sg * 128:oc * T2 + (sg + 1) * 128],
                                        qg[:, oc * T2:(oc + 1) * T2],
                                        start=(dc2 == 0), stop=(dc2 == 1))
                        att(instr)
                    stage(tq, "tensor", [(VE, kg_done)] + gw("po"), [(TE, 1)], ab)

                    def aev(vector, att, hp=hp):
                        vector.drain()
                        att(vector.tensor_tensor(
                            Asb[:, hp * 4 * T2:(hp + 1) * 4 * T2],
                            po[:, :], caus2[:, :], ALU.mult))
                    stage(tq, "vector", [(TE, TE.n)], [(VE, 1)], aev)
                    gset("po", VE)
                    aev_done = VE.n

                    for hh in range(2):
                        h = hp * 2 + hh

                        def ob(tensor, att, h=h, hh=hh):
                            for ec in range(4):
                                col = (hh * 4 + ec) * T2
                                first = True
                                for sg in range(2):
                                    tensor.matmul(
                                        pv[:, col:col + T2],
                                        vtm[:, sg * 2048 + h * 512 + ec * 128:sg * 2048 + h * 512 + (ec + 1) * 128],
                                        Asb[:, (h * 2 + sg) * T2:(h * 2 + sg + 1) * T2],
                                        start=first, stop=False)
                                    first = False
                                for dc2 in range(2):
                                    oc = 2 * h + dc2
                                    instr = tensor.matmul(
                                        pv[:, col:col + T2],
                                        Hbf[:, oc * 512 + ec * 128:oc * 512 + (ec + 1) * 128],
                                        qg[:, oc * T2:(oc + 1) * T2],
                                        start=False, stop=(dc2 == 1))
                            att(instr)
                        stage(tq, "tensor",
                              [(VE, aev_done)] + gw(f"pv{hh * 2}") + gw(f"pv{hh * 2 + 1}"),
                              [(TE, 1)], ob)

                        def oev(scalar, att, h=h, hh=hh, t=t):
                            scalar.drain()
                            att(scalar.activation(osb[:, h * 4 * T2:(h + 1) * 4 * T2],
                                                  pv[:, hh * 1024:(hh + 1) * 1024], ACT.Copy))
                        ow = [(TE, TE.n)]
                        if hp == 0 and hh == 0 and t > 0:
                            ow.append((GP, spill_done))
                        stage(tq, "scalar", ow, [(SE, 1)], oev)
                        gset(f"pv{hh * 2}", SE)
                        gset(f"pv{hh * 2 + 1}", SE)

                stage(tq, "gpsimd", [(SE, SE.n)], [(GP, 16)],
                      lambda g, att, t=t: att(g.dma_start(
                          out=io["o_sp"][:, t * T2:(t + 1) * T2].rearrange("(c p) n -> p c n", p=128),
                          in_=osb[:, :].rearrange("p (c n) -> p c n", c=16))))
                spill_done = GP.n

                if t + 1 < NT2:
                    exps_stage(t + 1)

                for hp in range(2):
                    def ubm(tensor, att, hp=hp):
                        for hh in range(2):
                            h = hp * 2 + hh
                            for dc2 in range(2):
                                oc = 2 * h + dc2
                                for sg in range(2):
                                    instr = tensor.matmul(
                                        pv[:, (hh * 2 + dc2) * 512:(hh * 2 + dc2 + 1) * 512],
                                        kHtm[:, sg * 1024 + oc * 128:sg * 1024 + (oc + 1) * 128],
                                        vtm[:, sg * 2048 + h * 512:sg * 2048 + (h + 1) * 512],
                                        start=(sg == 0), stop=(sg == 1))
                        att(instr)
                    stage(tq, "tensor",
                          [(VE, vtm_done)] + gw("pv0") + gw("pv1") + gw("pv2") + gw("pv3"),
                          [(TE, 1)], ubm)

                    def hup(vector, att, hp=hp, par=par):
                        vector.drain()
                        for hh in range(2):
                            h = hp * 2 + hh
                            for dc2 in range(2):
                                oc = 2 * h + dc2
                                instr = vector.scalar_tensor_tensor(
                                    Hf[:, oc * 512:(oc + 1) * 512],
                                    Hf[:, oc * 512:(oc + 1) * 512],
                                    ebl8[par][:, oc:oc + 1],
                                    pv[:, (hh * 2 + dc2) * 512:(hh * 2 + dc2 + 1) * 512],
                                    ALU.mult, ALU.add)
                        if hp == 1:
                            vector.drain()
                            instr = vector.tensor_copy(Hbf[:, :], Hf[:, :])
                        att(instr)
                    stage(tq, "vector", [(TE, TE.n)], [(VE, 1)], hup)
                    for p2 in range(4):
                        gset(f"pv{p2}", VE)

                tq.flush()

            @blk.vector
            def _(vector):
                vector.drain()
                vector.tensor_scalar(Hf[:, :], Hf[:, :], P["role"][:, :],
                                     None, ALU.mult, ALU.bypass)
                vector.drain()
                VE.inc(vector.tensor_copy(Hbf[:, :], Hf[:, :]))

            @blk.gpsimd
            def _(gpsimd):
                wt(gpsimd, VE)
                GP.inc(gpsimd.dma_start(out=io["st_snd"][:, :], in_=Hbf[:, :]), 16)
                GP.inc(gpsimd.dma_start(out=io["bl_sp"][:, :], in_=P["BL"][:, :]), 16)
                wt(gpsimd, GP)


def phase_p2a(nc, io, P, S, dev):
    LD, TE, VE, SE, GP, CC = (S[k] for k in ["LD", "TE", "VE", "SE", "GP", "CC"])
    LA, LB = S["LA"], S["LB"]

    with ExitStack() as ctx:
        e = ctx.enter_context
        wgb = e(nc.sbuf_tensor("wgb", [128, 8 * DF], BF16))
        wob = e(nc.sbuf_tensor("wob", [128, 16 * D], BF16))
        Spb = e(nc.sbuf_tensor("Spb", [128, 8 * DV], BF16))
        EXPOFF = e(nc.sbuf_tensor("EXPOFF", [128, 128], F32))
        offt = e(nc.sbuf_tensor("offt", [128, 16], F32))
        zt = [e(nc.sbuf_tensor(f"z2_{i}", [128, 8 * T2], BF16)) for i in range(2)]
        qgt = [e(nc.sbuf_tensor(f"qg2_{i}", [128, 8 * T2], BF16)) for i in range(2)]
        ot = [e(nc.sbuf_tensor(f"o2_{i}", [128, 16 * T2], BF16)) for i in range(2)]
        yt = e(nc.sbuf_tensor("y2", [128, 2 * D], F32))
        qgD = e(nc.sbuf_tensor("qgD", [128, 8 * T2], BF16))
        ofl = [e(nc.sbuf_tensor(f"ofl{i}", [128, 16 * T2], F32)) for i in range(2)]
        osq = e(nc.sbuf_tensor("osq", [128, 16 * T2], BF16))
        sg = [e(nc.sbuf_tensor(f"sg2_{i}", [128, 16 * T2], BF16)) for i in range(2)]
        ogb = [e(nc.sbuf_tensor(f"ogb{i}", [128, 16 * T2], BF16)) for i in range(2)]
        ut = e(nc.sbuf_tensor("ut", [128, 2 * D], F32))
        srow = [e(nc.sbuf_tensor(f"srow{i}", [1, 4 * T2], F32)) for i in range(2)]
        ubrow = srow[0]
        st3 = e(nc.sbuf_tensor("st3", [128, 8], F32))
        pv = e(nc.psum_tensor("pv_2a", [128, 2048], F32))
        po = e(nc.psum_tensor("po_2a", [128, 1024], F32))
        ps = e(nc.psum_tensor("ps_2a", [128, 1024], F32))

        guards = {}

        def gw(name):
            g = guards.get(name)
            return [g] if g is not None else []

        def gset(name, sem):
            guards[name] = (sem, sem.n)

        with nc.Block() as blk:
            tq = TQ(blk)

            WB = S["WB"]

            @blk.gpsimd
            def _(gpsimd):
                GP.inc(gpsimd.dma_start(
                    out=wgb[:, :].rearrange("p (c n) -> p c n", c=8),
                    in_=io["wg"][:, :].rearrange("(c p) n -> p c n", p=128)), 16)
                WB.inc(gpsimd.dma_start(
                    out=wob[:, :].rearrange("p (c n) -> p c n", c=16),
                    in_=io["wo"][:, :].rearrange("(c p) n -> p c n", p=128)), 16)
                if not dev:
                    CC.inc(gpsimd.collective_compute(
                        "AllReduce", ALU.add, replica_groups=PAIRS,
                        ins=[io["st_snd"].opt()], outs=[io["st_rcv"].opt()]))

            if dev:
                @blk.sync
                def _(sync):
                    LD.inc(sync.dma_start(out=P["BL"][:, :], in_=io["bl_sp"][:, :]), 16)

            wgb_done = GP.n

            @blk.vector
            def _(vector):
                wt(vector, WB)
                for fc in range(16):
                    instr = vector.tensor_scalar(
                        wob[:, fc * D:(fc + 1) * D], wob[:, fc * D:(fc + 1) * D],
                        P["gnwc"][:, fc % 4:fc % 4 + 1], None, ALU.mult, ALU.bypass)
                VE.inc(instr)
            w2a_done = VE.n

            @blk.vector
            def _(vector):
                if dev:
                    wt(vector, LD)
                vector.drain()
                for dc in range(8):
                    vector.tensor_tensor_scan(
                        offt[:, :], P["BL"][:, dc * 16:(dc + 1) * 16],
                        P["BL"][:, dc * 16:(dc + 1) * 16], 0.0, ALU.add, ALU.bypass)
                    vector.drain()
                    instr = vector.tensor_tensor(
                        EXPOFF[:, dc * 16:(dc + 1) * 16], offt[:, :],
                        P["BL"][:, dc * 16:(dc + 1) * 16], ALU.subtract)
                    vector.drain()
                VE.inc(instr)

            @blk.scalar
            def _(scalar):
                wt(scalar, VE)
                scalar.drain()
                SE.inc(scalar.activation(EXPOFF[:, :], EXPOFF[:, :], ACT.Exp,
                                         scale=-1.0 / 16.0))
            expoff_done = SE.n

            in_ready = [None] * NT2
            y_ready = [None] * NT2
            spill_done = GP.n

            order = [NT2 - 1] + list(range(NT2 - 1))

            @blk.sync
            def _(sync):
                s0 = order[0] * T2
                for dst, srcn, cd in [(zt[0], "z_sp", 8), (qgt[0], "qg_sp", 8),
                                      (ot[0], "o_sp", 16)]:
                    sync.dma_start(out=dst[:, :].rearrange("p (c n) -> p c n", c=cd),
                                   in_=io[srcn][:, s0:s0 + T2].rearrange("(c p) n -> p c n", p=128)).then_inc(LA.h, 16)
                sync.dma_start(out=yt[:, :].rearrange("p (g n) -> p g n", g=2),
                               in_=io["y_sp"][s0:s0 + T2, :].rearrange("(g p) n -> p g n", p=128)).then_inc(LA.h, 16)
                LA.n += 64
                in_ready[0] = (LA, LA.n)

            @blk.sync
            def _(sync):
                s0 = order[1] * T2
                for dst, srcn, cd in [(zt[1], "z_sp", 8), (qgt[1], "qg_sp", 8),
                                      (ot[1], "o_sp", 16)]:
                    sync.dma_start(out=dst[:, :].rearrange("p (c n) -> p c n", c=cd),
                                   in_=io[srcn][:, s0:s0 + T2].rearrange("(c p) n -> p c n", p=128)).then_inc(LB.h, 16)
                LB.n += 48
                in_ready[1] = (LB, LB.n)

            SP = S["SP"]

            @blk.gpsimd
            def _(gpsimd):
                if not dev:
                    wt(gpsimd, CC)
                SP.inc(gpsimd.dma_start(out=Spb[:, :], in_=io["st_rcv"][:, :]), 16)
            st_ld = SP.n

            @blk.vector
            def _(vector):
                wt(vector, SP, st_ld)
                vector.drain()
                VE.inc(vector.tensor_scalar(Spb[:, :], Spb[:, :], P["use"][:, :], None,
                                            ALU.mult, ALU.bypass))
            spb_done = VE.n

            qgd_done_arr = [None] * NT2
            silu3_arr = [None] * NT2
            cev3_arr = [None] * NT2
            cb_last_arr = [None] * NT2
            osq_done_arr = [None] * NT2
            inv_done_arr = [None] * NT2
            ub_done_arr = [None] * NT2
            ubsnd_holder = [None]
            ones_after_arr = [None] * NT2
            wo1_arr = [None] * NT2

            def inp_stage(tt):
                parw = tt % 2
                tid = order[tt]

                def qgd(vector, att, tid=tid, parw=parw):
                    vector.drain()
                    for oc in range(8):
                        instr = vector.tensor_scalar(
                            qgD[:, oc * T2:(oc + 1) * T2], qgt[parw][:, oc * T2:(oc + 1) * T2],
                            EXPOFF[:, oc * 16 + tid:oc * 16 + tid + 1], None,
                            ALU.mult, ALU.bypass)
                    att(instr)
                qw = [in_ready[tt], (SE, expoff_done)]
                if tt >= 1:
                    qw.append((TE, cb_last_arr[tt - 1]))
                stage(tq, "vector", qw, [(VE, 1)], qgd)
                qgd_done_arr[tt] = VE.n

                for q in range(4):
                    half = q % 2
                    pn0, pn1 = f"pv{half * 2}", f"pv{half * 2 + 1}"

                    def gb(tensor, att, q=q, half=half, parw=parw):
                        for f4 in range(4):
                            fc = q * 4 + f4
                            col = half * 1024 + f4 * T2
                            for ic in range(8):
                                instr = tensor.matmul(
                                    pv[:, col:col + T2],
                                    wgb[:, ic * DF + fc * 128:ic * DF + (fc + 1) * 128],
                                    zt[parw][:, ic * T2:(ic + 1) * T2],
                                    start=(ic == 0), stop=(ic == 7))
                        att(instr)
                    gbw = [in_ready[tt]] if q < 2 else []
                    if tt == 0 and q < 2:
                        gbw.append((GP, wgb_done))
                    stage(tq, "tensor", gbw + gw(pn0) + gw(pn1), [(TE, 1)], gb)

                    def silub(scalar, att, q=q, half=half, parw=parw):
                        scalar.drain()
                        att(scalar.activation(sg[parw][:, q * 4 * T2:(q + 1) * 4 * T2],
                                              pv[:, half * 1024:(half + 1) * 1024],
                                              ACT.Silu))
                    stage(tq, "scalar", [(TE, TE.n)], [(SE, 1)], silub)
                    gset(pn0, SE)
                    gset(pn1, SE)
                silu3_arr[tt] = SE.n

                for u in range(4):
                    h = u
                    half = u % 2
                    pn0, pn1 = f"pv{half * 2}", f"pv{half * 2 + 1}"

                    def cb(tensor, att, h=h, half=half):
                        for ec in range(4):
                            col = half * 1024 + ec * T2
                            for dc2 in range(2):
                                oc = 2 * h + dc2
                                instr = tensor.matmul(
                                    pv[:, col:col + T2],
                                    Spb[:, oc * DV + ec * 128:oc * DV + (ec + 1) * 128],
                                    qgD[:, oc * T2:(oc + 1) * T2],
                                    start=(dc2 == 0), stop=(dc2 == 1))
                        att(instr)
                    cw = [(VE, qgd_done_arr[tt])] if u < 2 else []
                    if tt == 0 and u < 2:
                        cw.append((VE, spb_done))
                    stage(tq, "tensor", cw + gw(pn0) + gw(pn1), [(TE, 1)], cb)

                    def cev(vector, att, h=h, half=half, parw=parw):
                        vector.drain()
                        att(vector.tensor_tensor(
                            ofl[parw][:, h * 4 * T2:(h + 1) * 4 * T2],
                            pv[:, half * 1024:(half + 1) * 1024],
                            ot[parw][:, h * 4 * T2:(h + 1) * 4 * T2], ALU.add))
                    stage(tq, "vector", [(TE, TE.n)], [(VE, 1)], cev)
                    gset(pn0, VE)
                    gset(pn1, VE)
                cev3_arr[tt] = VE.n
                cb_last_arr[tt] = TE.n

            def apply_stage(tt):
                parw = tt % 2
                tid = order[tt]

                def bcb(tensor, att, tt=tt, parw=parw):
                    for h in range(4):
                        instr = tensor.matmul(
                            ps[:, h * T2:(h + 1) * T2],
                            P["ones1"][0:1, 0:128], srow[parw][:, h * T2:(h + 1) * T2],
                            start=True, stop=True)
                    att(instr)
                stage(tq, "tensor", [(SE, inv_done_arr[tt])] + gw("ps"), [(TE, 1)], bcb)
                bcast_done = TE.n

                def onb(vector, att, parw=parw):
                    instr = None
                    for h in range(4):
                        vector.drain()
                        for ec in range(4):
                            idx = (h * 4 + ec) * T2
                            instr = vector.tensor_tensor(ogb[parw][:, idx:idx + T2],
                                                         ogb[parw][:, idx:idx + T2],
                                                         ps[:, h * T2:(h + 1) * T2], ALU.mult)
                    att(instr)
                stage(tq, "vector", [(TE, bcast_done)], [(VE, 1)], onb)
                og_done = VE.n
                gset("ps", VE)

                def wo0(tensor, att, parw=parw):
                    for j in range(2):
                        for fc in range(16):
                            instr = tensor.matmul(
                                po[:, j * 512:(j + 1) * 512],
                                ogb[parw][:, fc * T2 + 0 * 128:fc * T2 + 1 * 128],
                                wob[:, fc * D + j * 512:fc * D + (j + 1) * 512],
                                start=(fc == 0), stop=(fc == 15))
                    att(instr)
                wow = [(VE, og_done)] + gw("po")
                if tt == 0:
                    wow.append((VE, w2a_done))
                stage(tq, "tensor", wow, [(TE, 1)], wo0)

                def wo1(tensor, att, parw=parw):
                    for j in range(2):
                        for fc in range(16):
                            instr = tensor.matmul(
                                pv[:, j * 512:(j + 1) * 512],
                                ogb[parw][:, fc * T2 + 1 * 128:fc * T2 + 2 * 128],
                                wob[:, fc * D + j * 512:fc * D + (j + 1) * 512],
                                start=(fc == 0), stop=(fc == 15))
                    att(instr)
                stage(tq, "tensor", gw("pv0") + gw("pv1"), [(TE, 1)], wo1)
                wo1_done = TE.n
                wo1_arr[tt] = TE.n

                def ub_(vector, att):
                    vector.drain()
                    vector.tensor_tensor(ut[:, 0:D], po[:, 0:D], yt[:, 0:D], ALU.add)
                    att(vector.tensor_tensor(ut[:, D:2 * D], pv[:, 0:D],
                                             yt[:, D:2 * D], ALU.add))
                uw = [(TE, wo1_done)]
                if tt > 0:
                    uw.append(y_ready[tt])
                    uw.append((GP, spill_done))
                stage(tq, "vector", uw, [(VE, 1)], ub_)
                gset("po", VE)
                gset("pv0", VE)
                gset("pv1", VE)
                ub_done_arr[tt] = VE.n

                def l3red(vector, att):
                    vector.drain()
                    for tg in range(2):
                        vector.reduce_sum(st3[:, tg:tg + 1], ut[:, tg * D:(tg + 1) * D],
                                          axis=AX.X)
                    vector.drain()
                    att(vector.tensor_scalar(st3[:, 0:2], st3[:, 0:2], 1.0 / D, None,
                                             ALU.mult, ALU.bypass))
                stage(tq, "vector", [], [(VE, 1)], l3red)

                def l3sq(scalar, att):
                    for tg in range(2):
                        scalar.drain()
                        instr = scalar.activation(osq[:, 0:1024],
                                                  ut[:, tg * D:(tg + 1) * D],
                                                  ACT.Square, accum_out=st3[:, 2 + tg:3 + tg])
                    att(instr)
                onesg = ones_after_arr[tt + 1 if tt + 1 < NT2 else tt]
                stage(tq, "scalar", [(VE, VE.n), (TE, onesg)], [(SE, 1)], l3sq)

                def l3var(vector, att):
                    vector.drain()
                    vector.tensor_tensor(st3[:, 4:6], st3[:, 0:2], st3[:, 0:2], ALU.mult)
                    vector.drain()
                    att(vector.scalar_tensor_tensor(
                        st3[:, 4:6], st3[:, 2:4], 1.0 / D, st3[:, 4:6],
                        ALU.mult, ALU.subtract))
                stage(tq, "vector", [(SE, SE.n)], [(VE, 1)], l3var)

                stage(tq, "scalar", [(VE, VE.n)], [(SE, 1)],
                      lambda scalar, att: (scalar.drain(),
                                           att(scalar.activation(st3[:, 6:8], st3[:, 4:6],
                                                                 ACT.Sqrt, bias=P["ceps"]))))

                def l3fin(vector, att, tid=tid):
                    vector.drain()
                    vector.reciprocal(st3[:, 6:8], st3[:, 6:8])
                    vector.drain()
                    vector.tensor_copy(P["l3m"][:, 2 * tid:2 * tid + 2], st3[:, 0:2])
                    att(vector.tensor_copy(P["l3i"][:, 2 * tid:2 * tid + 2], st3[:, 6:8]))
                stage(tq, "vector", [(SE, SE.n)], [(VE, 1)], l3fin)

                stage(tq, "gpsimd", [(VE, VE.n)], [(GP, 16)],
                      lambda g, att, tid=tid: att(g.dma_start(
                          out=io["u_sp"][tid * T2:(tid + 1) * T2, :].rearrange("(g p) n -> p g n", p=128),
                          in_=ut[:, :].rearrange("p (g n) -> p g n", g=2))))

            inp_stage(0)

            for t in range(NT2):
                par = t % 2

                if t + 2 < NT2:
                    psem = LB if (t + 2) % 2 == 1 else LA
                    pw = [(TE, TE.n), (VE, VE.n)]
                    s0 = order[t + 2] * T2
                    for dst, srcn, cd in [(zt[par], "z_sp", 8), (qgt[par], "qg_sp", 8),
                                          (ot[par], "o_sp", 16)]:
                        stage(tq, "sync", pw, [(psem, 16)],
                              lambda s, att, dst=dst, srcn=srcn, cd=cd, s0=s0: att(
                                  s.dma_start(out=dst[:, :].rearrange("p (c n) -> p c n", c=cd),
                                              in_=io[srcn][:, s0:s0 + T2].rearrange("(c p) n -> p c n", p=128))))
                        pw = []
                    in_ready[t + 2] = (psem, psem.n)

                def osqb(scalar, att, par=par):
                    scalar.drain()
                    att(scalar.activation(osq[:, :], ofl[par][:, :], ACT.Square))
                stage(tq, "scalar", [(VE, cev3_arr[t])], [(SE, 1)], osqb)
                osq_done_arr[t] = SE.n

                def oflg(vector, att, par=par):
                    vector.drain()
                    att(vector.tensor_tensor(ogb[par][:, :], ofl[par][:, :],
                                             sg[par][:, :], ALU.mult))
                ow = [(SE, osq_done_arr[t])]
                if t >= 2:
                    ow.append((TE, wo1_arr[t - 2]))
                stage(tq, "vector", ow, [(VE, 1)], oflg)

                if t + 1 < NT2:
                    inp_stage(t + 1)

                def onesmm(tensor, att):
                    for h in range(4):
                        for ec in range(4):
                            instr = tensor.matmul(
                                ps[0:1, h * T2:(h + 1) * T2],
                                P["onesb"][:, :],
                                osq[:, (h * 4 + ec) * T2:(h * 4 + ec + 1) * T2],
                                start=(ec == 0), stop=(ec == 3))
                    att(instr)
                stage(tq, "tensor", [(SE, osq_done_arr[t])] + gw("ps"), [(TE, 1)], onesmm)
                ones_after_arr[t] = TE.n

                def srowb(vector, att, par=par):
                    vector.drain()
                    att(vector.tensor_scalar(srow[par][:, :], ps[0:1, :], 1.0 / DV, EPS,
                                             ALU.mult, ALU.add))
                sw = [(TE, TE.n)]
                if t == 2 and ubsnd_holder[0] is not None:
                    sw.append((GP, ubsnd_holder[0]))
                stage(tq, "vector", sw, [(VE, 1)], srowb)
                gset("ps", VE)

                def rsq(scalar, att, par=par):
                    scalar.drain()
                    scalar.activation(srow[par][:, :], srow[par][:, :], ACT.Ln)
                    scalar.drain()
                    att(scalar.activation(srow[par][:, :], srow[par][:, :], ACT.Exp,
                                          scale=-0.5))
                stage(tq, "scalar", [(VE, VE.n)], [(SE, 1)], rsq)
                inv_done_arr[t] = SE.n

                if t > 0:
                    apply_stage(t - 1)
                    spill_done = GP.n
                    if t == 1:
                        stage(tq, "gpsimd", [(VE, VE.n)], [(GP, 16)],
                              lambda g, att: att(g.dma_start(out=ubrow[:, :],
                                                             in_=ut[127:128, D:2 * D])))
                        spill_done = GP.n

                        def ubm(vector, att):
                            vector.drain()
                            att(vector.tensor_scalar(ubrow[:, :], ubrow[:, :],
                                                     P["role"][0:1, 0:1], None,
                                                     ALU.mult, ALU.bypass))
                        stage(tq, "vector", [(GP, GP.n)], [(VE, 1)], ubm)

                        stage(tq, "gpsimd", [(VE, VE.n)], [(GP, 16)],
                              lambda g, att: att(g.dma_start(out=io["ub_snd"][:, :],
                                                             in_=ubrow[:, :])))
                        ubsnd_val = GP.n
                        ubsnd_holder[0] = GP.n
                        spill_done = GP.n
                        if not dev:
                            stage(tq, "gpsimd", [(GP, ubsnd_val)], [(CC, 1)],
                                  lambda g, att: att(g.collective_compute(
                                      "AllReduce", ALU.add, replica_groups=PAIRS,
                                      ins=[io["ub_snd"].opt()], outs=[io["ub_rcv"].opt()])))
                    s0 = order[t] * T2
                    stage(tq, "sync", [(VE, ub_done_arr[t - 1])], [(LD, 16)],
                          lambda s, att, s0=s0: att(
                              s.dma_start(out=yt[:, :].rearrange("p (g n) -> p g n", g=2),
                                          in_=io["y_sp"][s0:s0 + T2, :].rearrange("(g p) n -> p g n", p=128))))
                    y_ready[t] = (LD, LD.n)
                tq.flush()

            apply_stage(NT2 - 1)
            tq.flush()

            @blk.gpsimd
            def _(gpsimd):
                wt(gpsimd, VE)
                GP.inc(gpsimd.dma_start(out=io["l3_sp"][:, 0:32], in_=P["l3m"][:, :]), 16)
                GP.inc(gpsimd.dma_start(out=io["l3_sp"][:, 32:64], in_=P["l3i"][:, :]), 16)
                wt(gpsimd, GP)


def make_consts():
    consts = {}
    consts["identb"] = np.eye(128, dtype=np.float32).astype(ml_dtypes.bfloat16)
    caus = np.zeros((128, 2 * C), dtype=np.float32)
    for sg_ in range(2):
        for p in range(128):
            caus[p, sg_ * C + sg_ * 128 + p:(sg_ + 1) * C] = 1.0
    consts["caus"] = caus.astype(ml_dtypes.bfloat16)
    return consts


def make_in_maps(inputs):
    x = np.ascontiguousarray(inputs["L0_attn_x"], dtype=np.float32)
    w = {
        "wq": inputs["L0_attn_Wq"], "wk": inputs["L0_attn_Wk"],
        "wv": inputs["L0_attn_Wv"], "wg": inputs["L0_attn_Wg"],
        "gw1": inputs["L0_attn_gk_w1"], "gw2": inputs["L0_attn_gk_w2"],
        "gb2": np.reshape(inputs["L0_attn_gk_b2"], (1, D)),
        "gnw": np.reshape(inputs["L0_attn_g_norm_w"], (1, DV)),
        "wo": inputs["L0_attn_Wo"],
        "fr": inputs["L0_ffn_W_r"], "fk": inputs["L0_ffn_W_k"],
        "fv": inputs["L0_ffn_W_v"],
        "mur": np.reshape(inputs["L0_ffn_mu_r"], (1, D)),
        "muk": np.reshape(inputs["L0_ffn_mu_k"], (1, D)),
    }
    w = {k: np.ascontiguousarray(v, dtype=np.float32) for k, v in w.items()}
    w.update(make_consts())

    in_maps = []
    for core in range(8):
        b, half = core // 2, core % 2
        t0 = half * SEG
        pad = np.zeros((128, D), dtype=np.float32)
        if half == 1:
            pad[127] = x[b, t0 - 1]
        m = dict(w)
        m["x"] = np.concatenate([pad, x[b, t0:t0 + SEG]], axis=0)
        m["role"] = np.full((128, 1), 1.0 if half == 0 else 0.0, np.float32)
        m["use"] = np.full((128, 1), 1.0 if half == 1 else 0.0, np.float32)
        in_maps.append(m)
    return in_maps


def kernel(**inputs):
    nc = build_program()
    in_maps = make_in_maps(inputs)
    res = run_bass_kernel_spmd(nc, in_maps, core_ids=list(range(8)))
    out = np.empty((4, 8192, D), dtype=np.float32)
    for core in range(8):
        b, half = core // 2, core % 2
        out[b, half * SEG:(half + 1) * SEG] = res.results[core]["out"]
    return out


# version 16
# speedup vs baseline: 1.0432x; 1.0432x over previous
import numpy as np
from contextlib import ExitStack

import concourse.bass as bass
import concourse.mybir as mybir
from concourse.bass_utils import run_bass_kernel_spmd
import ml_dtypes

F32 = mybir.dt.float32
BF16 = mybir.dt.bfloat16
ALU = mybir.AluOpType
ACT = mybir.ActivationFunctionType
AX = mybir.AxisListType

SEG = 4096
D = 1024
NH = 4
DK = 256
DV = 512
DF = 2048
C = 256
T1 = 512
NT1 = SEG // T1
T2 = 256
NT2 = SEG // T2
LN16 = float(np.log(16.0))
EPS = 1e-5
PAIRS = [[0, 1], [2, 3], [4, 5], [6, 7]]


class Sem:
    def __init__(self, h):
        self.h = h
        self.n = 0

    def reserve(self, k=1):
        self.n += k
        return self.n

    def inc(self, instr, k=1):
        instr.then_inc(self.h, k)
        self.n += k
        return self.n


def wt(eng, sem, val=None):
    eng.wait_ge(sem.h, sem.n if val is None else val)


class TQ:
    ORDER = ["sync", "gpsimd", "tensor", "scalar", "vector"]

    def __init__(self, blk):
        self.blk = blk
        self.q = {k: [] for k in self.ORDER}

    def add(self, eng, fn):
        self.q[eng].append(fn)

    def flush(self):
        for name in self.ORDER:
            fns = self.q[name]
            if fns:
                getattr(self.blk, name)(
                    lambda eng, fns=tuple(fns): [f(eng) for f in fns])
            self.q[name] = []


def stage(tq, engname, waits, incs, body):
    for s, k in incs:
        s.reserve(k)
    waits = tuple(waits)
    incs = tuple(incs)

    def f(eng):
        for s, v in waits:
            eng.wait_ge(s.h, v)

        def att(instr):
            for s, k in incs:
                instr.then_inc(s.h, k)
            return instr
        body(eng, att)
    tq.add(engname, f)


def build_program(dev_phases=None):
    dev = dev_phases is not None
    phases = dev_phases or ("p1a", "p1b", "p2a", "p2b")

    nc = bass.Bass("TRN2", target_bir_lowering=False)

    def dram_in(name, shape, dt=F32):
        return nc.dram_tensor(name, shape, dt, kind="ExternalInput").ap()

    def dram_out(name, shape, dt=F32):
        return nc.dram_tensor(name, shape, dt, kind="ExternalOutput").ap()

    def spill(name, shape, dt, produced):
        if not dev:
            return nc.dram_tensor(name, shape, dt).ap()
        return dram_out(name, shape, dt) if produced in phases else dram_in(name, shape, dt)

    io = {}
    io["x"] = dram_in("x", [128 + SEG, D])
    for n, s in [("wq", [D, D]), ("wk", [D, D]), ("wv", [D, DF]), ("wg", [D, DF]),
                 ("gw1", [D, 16]), ("gw2", [16, D]), ("gb2", [1, D]),
                 ("gnw", [1, DV]), ("wo", [DF, D]), ("fr", [D, D]),
                 ("fk", [D, D]), ("fv", [D, D]), ("mur", [1, D]), ("muk", [1, D])]:
        io[n] = dram_in(n, s)
    io["identb"] = dram_in("identb", [128, 128], BF16)
    io["caus"] = dram_in("caus", [128, 2 * C], BF16)
    io["role"] = dram_in("role", [128, 1])
    io["use"] = dram_in("use", [128, 1])
    io["out"] = dram_out("out", [SEG, D])

    io["y_sp"] = spill("y_sp", [SEG, D], F32, "p1a")
    io["z_sp"] = spill("z_sp", [D, SEG], BF16, "p1a")
    io["qg_sp"] = spill("qg_sp", [D, SEG], BF16, "p1b")
    io["o_sp"] = spill("o_sp", [DF, SEG], BF16, "p1b")
    io["u_sp"] = spill("u_sp", [SEG, D], F32, "p2a")
    io["bl_sp"] = spill("bl_sp", [128, 128], F32, "p1b")
    io["l3_sp"] = spill("l3_sp", [128, 64], F32, "p2a")
    io["st_snd"] = spill("st_snd", [128, 8 * DV], BF16, "p1b")
    io["ub_snd"] = spill("ub_snd", [1, D], F32, "p2a")
    if dev:
        io["st_rcv"] = dram_in("st_rcv", [128, 8 * DV], BF16)
        io["ub_rcv"] = dram_in("ub_rcv", [1, D])
    else:
        io["st_rcv"] = nc.dram_tensor("st_rcv", [128, 8 * DV], BF16).ap()
        io["ub_rcv"] = nc.dram_tensor("ub_rcv", [1, D], F32).ap()

    with ExitStack() as top:
        et = top.enter_context
        P = {}
        P["identb"] = et(nc.sbuf_tensor("p_identb", [128, 128], BF16))
        P["role"] = et(nc.sbuf_tensor("p_role", [128, 1], F32))
        P["use"] = et(nc.sbuf_tensor("p_use", [128, 1], F32))
        P["murs"] = et(nc.sbuf_tensor("p_murs", [128, 8], F32))
        P["muks"] = et(nc.sbuf_tensor("p_muks", [128, 8], F32))
        P["ones1"] = et(nc.sbuf_tensor("p_ones1", [1, 512], F32))
        P["onesb"] = et(nc.sbuf_tensor("p_onesb", [128, 1], BF16))
        P["gnwc"] = et(nc.sbuf_tensor("p_gnwc", [128, 4], F32))
        P["BL"] = et(nc.sbuf_tensor("p_BL", [128, 128], F32))
        P["l3m"] = et(nc.sbuf_tensor("p_l3m", [128, 32], F32))
        P["l3i"] = et(nc.sbuf_tensor("p_l3i", [128, 32], F32))
        cvals = et(nc.sbuf_tensor("p_cvals", [128, 3], F32))
        P["ceps"] = cvals[:, 0:1]
        P["cone"] = cvals[:, 1:2]
        P["cln16n"] = cvals[:, 2:3]

        S = {k: Sem(et(nc.semaphore("s_" + k)))
             for k in ["LD", "LA", "LB", "TE", "VE", "SE", "GP", "GC", "CC", "SP", "ZP", "WB",
                       "WC", "W1", "W2", "W3"]}

        with nc.Block() as blk:
            @blk.sync
            def _(sync):
                for dst, src in [(P["identb"], io["identb"]),
                                 (P["role"], io["role"]), (P["use"], io["use"])]:
                    S["LD"].inc(sync.dma_start(out=dst[:, :], in_=src[:, :]), 16)
                for dst, src in [(P["murs"], io["mur"]), (P["muks"], io["muk"])]:
                    S["LD"].inc(sync.dma_start(
                        out=dst[:, :], in_=src[0, :].rearrange("(c p) -> p c", p=128),
                        allow_slow_non_contiguous=True), 16)
                S["LD"].inc(sync.dma_start(
                    out=P["gnwc"][:, :], in_=io["gnw"][0, :].rearrange("(c p) -> p c", p=128),
                    allow_slow_non_contiguous=True), 16)

            @blk.vector
            def _(vector):
                wt(vector, S["LD"])
                vector.memset(P["ones1"][:, :], 1.0)
                vector.memset(P["onesb"][:, :], 1.0)
                vector.memset(P["BL"][:, :], 0.0)
                vector.memset(P["l3m"][:, :], 0.0)
                vector.memset(P["l3i"][:, :], 0.0)
                vector.memset(P["ceps"], EPS)
                vector.memset(P["cone"], 1.0)
                vector.memset(P["cln16n"], -LN16)

        if "p1a" in phases:
            phase_ffn(nc, io, P, S, which="1a")
        if "p1b" in phases:
            phase_p1b(nc, io, P, S)
        if "p2a" in phases:
            phase_p2a(nc, io, P, S, dev)
        if "p2b" in phases:
            phase_ffn(nc, io, P, S, which="2b", dev=dev)

    return nc


def phase_ffn(nc, io, P, S, which, dev=False):
    p2b = which == "2b"
    tag = which
    LD, TE, VE, SE, GP = (S[k] for k in ["LD", "TE", "VE", "SE", "GP"])
    GC, ZP = S["GC"], S["ZP"]
    LA, LB = S["LA"], S["LB"]

    with ExitStack() as ctx:
        e = ctx.enter_context
        ffnT = {w: e(nc.sbuf_tensor(f"ffnT_{w}_{tag}", [128, 8 * D], BF16))
                for w in ["fr", "fk", "fv"]}
        pstA = e(nc.psum_tensor(f"pstA_{tag}", [128, 1024], BF16))
        pstB = e(nc.psum_tensor(f"pstB_{tag}", [128, 1024], BF16))
        guards = {}

        def gw(name):
            g = guards.get(name)
            return [g] if g is not None else []

        def gset(name, sem):
            guards[name] = (sem, sem.n)

        with ExitStack() as pctx:
            wstage = [pctx.enter_context(
                nc.sbuf_tensor(f"wstage{i}_{tag}", [128, 8 * D], BF16)) for i in range(2)]
            WS = [S["W1"], S["W2"], S["W3"]]
            with nc.Block() as blk:
                tq = TQ(blk)
                if p2b:
                    stage(tq, "sync", [(GP, GP.n)], [(LD, 16)], lambda s, att: att(
                        s.dma_start(out=P["l3m"][:, :], in_=io["l3_sp"][:, 0:32])))
                    stage(tq, "sync", [], [(LD, 16)], lambda s, att: att(
                        s.dma_start(out=P["l3i"][:, :], in_=io["l3_sp"][:, 32:64])))
                tr_done = [None, None, None]
                for wi, w in enumerate(["fr", "fk", "fv"]):
                    ws = wstage[wi % 2]
                    if wi == 0:
                        s_lo, s_hi = WS[0], WS[1]
                        stage(tq, "gpsimd", [], [(s_lo, 16)],
                              lambda g, att, w=w, ws=ws: att(g.dma_start(
                                  out=ws[:, 0:4096].rearrange("p (c n) -> p c n", c=4),
                                  in_=io[w][0:512, :].rearrange("(c p) n -> p c n", p=128))))
                        lo_ld = s_lo.n
                        stage(tq, "gpsimd", [], [(s_hi, 16)],
                              lambda g, att, w=w, ws=ws: att(g.dma_start(
                                  out=ws[:, 4096:8192].rearrange("p (c n) -> p c n", c=4),
                                  in_=io[w][512:1024, :].rearrange("(c p) n -> p c n", p=128))))
                        hi_ld = s_hi.n
                    else:
                        s_lo = s_hi = WS[2] if wi == 1 else WS[0]
                        dw = [(TE, tr_done[0])] if wi == 2 else []
                        stage(tq, "gpsimd", dw, [(s_lo, 16)],
                              lambda g, att, w=w, ws=ws: att(g.dma_start(
                                  out=ws[:, :].rearrange("p (c n) -> p c n", c=8),
                                  in_=io[w][:, :].rearrange("(c p) n -> p c n", p=128))))
                        lo_ld = hi_ld = s_lo.n
                    for dc in range(8):
                        ps = pstA if dc % 2 == 0 else pstB
                        pname = "pstA" if dc % 2 == 0 else "pstB"

                        def trbody(tensor, att, w=w, dc=dc, ps=ps, ws=ws):
                            for cc in range(8):
                                instr = tensor.transpose(
                                    ps[:, cc * 128:(cc + 1) * 128],
                                    ws[:, dc * 1024 + cc * 128:dc * 1024 + (cc + 1) * 128],
                                    P["identb"][:, :])
                            att(instr)
                        trw = []
                        if dc == 0:
                            trw = [(s_lo, lo_ld)]
                        elif dc == 4 and wi == 0:
                            trw = [(s_hi, hi_ld)]
                        stage(tq, "tensor", trw + gw(pname), [(TE, 1)], trbody)

                        def evbody(scalar, att, w=w, dc=dc, ps=ps):
                            scale = 0.25 if w == "fv" else 1.0
                            out3 = ffnT[w][:, :].rearrange("p (c n) -> p c n", c=8)[:, :, dc * 128:(dc + 1) * 128]
                            att(scalar.activation(out3, ps[:, :].rearrange("p (c n) -> p c n", c=8),
                                                  ACT.Copy, scale=scale))
                        stage(tq, "scalar", [(TE, TE.n)], [(SE, 1)], evbody)
                        gset(pname, SE)
                    tr_done[wi] = TE.n
                tq.flush()

        pk = e(nc.psum_tensor(f"pk_{tag}", [128, 2048], F32))
        if not p2b:
            pzA = e(nc.psum_tensor(f"pzA_{tag}", [128, 1024], BF16))
            pzB = e(nc.psum_tensor(f"pzB_{tag}", [128, 1024], BF16))
        xt = [e(nc.sbuf_tensor(f"xt{i}_{tag}", [128, T1 * 8], F32)) for i in range(2)]
        scrap = e(nc.sbuf_tensor(f"scrap_{tag}", [128, 1024], F32))
        st1 = e(nc.sbuf_tensor(f"st1_{tag}", [128, 16], F32))
        st2 = e(nc.sbuf_tensor(f"st2_{tag}", [128, 16], F32))
        ln1tm = [e(nc.sbuf_tensor(f"ln1tm{i}_{tag}", [128, 4 * 1024], BF16)) for i in range(2)]
        lnfm = [e(nc.sbuf_tensor(f"lnfm{i}_{tag}", [128, 8 * 513], BF16)) for i in range(2)]
        xr = [e(nc.sbuf_tensor(f"xr{i}_{tag}", [128, 8 * 512], BF16)) for i in range(2)]
        xk = [e(nc.sbuf_tensor(f"xk{i}_{tag}", [128, 8 * 512], BF16)) for i in range(2)]
        kact = e(nc.sbuf_tensor(f"kact_{tag}", [128, 8 * 512], BF16))
        rth = e(nc.sbuf_tensor(f"rth_{tag}", [128, 4 * 1024], BF16))
        yt = e(nc.sbuf_tensor(f"yt_{tag}", [128, 4 * 1024], F32))
        bx = yt
        if not p2b:
            ztm = e(nc.sbuf_tensor(f"ztm_{tag}", [128, 4 * 1024], BF16))
            zfm = e(nc.sbuf_tensor(f"zfm_{tag}", [128, 8 * 512], BF16))
        lnfm3s = [lnfm[i][:, :].rearrange("p (c n) -> p c n", n=513) for i in range(2)]

        with nc.Block() as blk:
            tq = TQ(blk)
            xt_ready = [None] * NT1

            @blk.sync
            def _(sync):
                if not p2b:
                    LD.inc(sync.dma_start(out=bx[:, 0:1024], in_=io["x"][0:128, :]), 16)
                src = io["u_sp"] if p2b else io["x"]
                off = 0 if p2b else 128
                xt_ready[0] = (LA, LA.inc(sync.dma_start(
                    out=xt[0][:, :].rearrange("p (g n) -> p g n", g=4),
                    in_=src[off:off + T1, :].rearrange("(g p) n -> p g n", p=128)), 16))

            if p2b:
                @blk.vector
                def _(vector):
                    VE.inc(vector.memset(bx[:, 0:1024], 0.0))

                @blk.gpsimd
                def _(gpsimd):
                    wt(gpsimd, VE)
                    if not dev:
                        wt(gpsimd, S["CC"])
                    GP.inc(gpsimd.dma_start(out=bx[127:128, 0:1024], in_=io["ub_rcv"][:, :]), 16)

            @blk.vector
            def _(vector):
                if p2b:
                    wt(vector, GP)
                else:
                    wt(vector, LD)
                vector.reduce_sum(st1[:, 0:1], bx[:, 0:1024], axis=AX.X)
                vector.drain()
                VE.inc(vector.tensor_scalar(st1[:, 0:1], st1[:, 0:1], 1.0 / D, None,
                                            ALU.mult, ALU.bypass))

            @blk.scalar
            def _(scalar):
                wt(scalar, VE)
                SE.inc(scalar.activation(scrap[:, :], bx[:, 0:1024], ACT.Square,
                                         accum_out=st1[:, 4:5]))

            @blk.vector
            def _(vector):
                wt(vector, SE)
                vector.drain()
                vector.tensor_tensor(st1[:, 8:9], st1[:, 0:1], st1[:, 0:1], ALU.mult)
                vector.drain()
                VE.inc(vector.scalar_tensor_tensor(
                    st1[:, 8:9], st1[:, 4:5], 1.0 / D, st1[:, 8:9],
                    ALU.mult, ALU.subtract))

            @blk.scalar
            def _(scalar):
                wt(scalar, VE)
                SE.inc(scalar.activation(st1[:, 12:13], st1[:, 8:9], ACT.Sqrt, bias=P["ceps"]))

            @blk.vector
            def _(vector):
                wt(vector, SE)
                vector.drain()
                vector.reciprocal(st1[:, 12:13], st1[:, 12:13])
                vector.drain()
                VE.inc(vector.tensor_scalar(ln1tm[0][:, 0:1024], bx[:, 0:1024],
                                            st1[:, 0:1], st1[:, 12:13],
                                            ALU.subtract, ALU.mult))

            @blk.tensor
            def _(tensor):
                wt(tensor, VE)
                for g_ in gw("pstA"):
                    tensor.wait_ge(g_[0].h, g_[1])
                for cc in range(8):
                    instr = tensor.transpose(pstA[:, cc * 128:(cc + 1) * 128],
                                             ln1tm[0][:, cc * 128:(cc + 1) * 128],
                                             P["identb"][:, :])
                TE.inc(instr)

            @blk.vector
            def _(vector):
                wt(vector, TE)
                src = pstA[:, :].rearrange("p (c n) -> p c n", c=8)[:, :, 127:128]
                dst = lnfm3s[0][:, :, 0:1]
                vector.drain()
                vector.tensor_copy(dst, src)
                vector.drain()
                VE.inc(vector.tensor_scalar(dst, dst, P["use"][:, :], None,
                                            ALU.mult, ALU.bypass))
            gset("pstA", VE)

            def ln1_chain(tt):
                pp = tt % 2
                if not p2b:
                    def v_stats(vector, att, pp=pp):
                        vector.drain()
                        for g in range(4):
                            vector.reduce_sum(st1[:, g:g + 1],
                                              xt[pp][:, g * 1024:(g + 1) * 1024], axis=AX.X)
                        vector.drain()
                        att(vector.tensor_scalar(st1[:, 0:4], st1[:, 0:4], 1.0 / D,
                                                 None, ALU.mult, ALU.bypass))
                    stage(tq, "vector", [xt_ready[tt]], [(VE, 1)], v_stats)

                    def s_sq(scalar, att, pp=pp):
                        for g in range(4):
                            scalar.drain()
                            instr = scalar.activation(scrap[:, :],
                                                      xt[pp][:, g * 1024:(g + 1) * 1024],
                                                      ACT.Square, accum_out=st1[:, 4 + g:5 + g])
                        att(instr)
                    stage(tq, "scalar", [(VE, VE.n)], [(SE, 1)], s_sq)

                    def v_var(vector, att):
                        vector.drain()
                        vector.tensor_tensor(st1[:, 8:12], st1[:, 0:4], st1[:, 0:4], ALU.mult)
                        vector.drain()
                        att(vector.scalar_tensor_tensor(
                            st1[:, 8:12], st1[:, 4:8], 1.0 / D, st1[:, 8:12],
                            ALU.mult, ALU.subtract))
                    stage(tq, "vector", [(SE, SE.n)], [(VE, 1)], v_var)

                    stage(tq, "scalar", [(VE, VE.n)], [(SE, 1)],
                          lambda scalar, att: att(scalar.activation(
                              st1[:, 12:16], st1[:, 8:12], ACT.Sqrt, bias=P["ceps"])))

                    def v_apply(vector, att, pp=pp):
                        vector.drain()
                        vector.reciprocal(st1[:, 12:16], st1[:, 12:16])
                        vector.drain()
                        for g in range(4):
                            instr = vector.tensor_scalar(
                                ln1tm[pp][:, g * 1024:(g + 1) * 1024],
                                xt[pp][:, g * 1024:(g + 1) * 1024],
                                st1[:, g:g + 1], st1[:, 12 + g:13 + g],
                                ALU.subtract, ALU.mult)
                        att(instr)
                    stage(tq, "vector", [(SE, SE.n), (TE, TE.n)], [(VE, 1)], v_apply)
                    ln1_done[tt] = VE.n
                else:
                    def v_apply(vector, att, tt=tt, pp=pp):
                        vector.drain()
                        for g in range(4):
                            gg = tt * 4 + g
                            instr = vector.tensor_scalar(
                                ln1tm[pp][:, g * 1024:(g + 1) * 1024],
                                xt[pp][:, g * 1024:(g + 1) * 1024],
                                P["l3m"][:, gg:gg + 1], P["l3i"][:, gg:gg + 1],
                                ALU.subtract, ALU.mult)
                        att(instr)
                    stage(tq, "vector", [xt_ready[tt], (TE, TE.n), (LD, LD.n)],
                          [(VE, 1)], v_apply)
                    ln1_done[tt] = VE.n

            ln1_done = [None] * NT1
            mix_done_arr = [None] * NT1
            mixk_done_arr = [None] * NT1
            yadd_done_arr = [None] * NT1
            ybody_last = [None] * NT1
            v2a_done = [None] * NT1
            yspill_done = [None] * NT1
            zspill_prev = [ZP.n]

            def mix_group(tt):
                pp = tt % 2
                for g in range(4):
                    ps = pstA if g % 2 == 0 else pstB
                    pname = "pstA" if g % 2 == 0 else "pstB"

                    def trbody(tensor, att, g=g, ps=ps, pp=pp):
                        for cc in range(8):
                            instr = tensor.transpose(
                                ps[:, cc * 128:(cc + 1) * 128],
                                ln1tm[pp][:, g * 1024 + cc * 128:g * 1024 + (cc + 1) * 128],
                                P["identb"][:, :])
                        att(instr)
                    stage(tq, "tensor", [(VE, ln1_done[tt])] + gw(pname), [(TE, 1)], trbody)

                    def evbody(scalar, att, g=g, ps=ps, pp=pp):
                        scalar.drain()
                        dst = lnfm3s[pp][:, :, 1 + g * 128:1 + (g + 1) * 128]
                        att(scalar.activation(dst, ps[:, :].rearrange("p (c n) -> p c n", c=8),
                                              ACT.Copy))
                    evw = [(TE, TE.n)]
                    if g == 0 and tt >= 2:
                        evw.append((VE, mix_done_arr[tt - 2]))
                    stage(tq, "scalar", evw, [(SE, 1)], evbody)
                    gset(pname, SE)

                def v_mix_k(vector, att, pp=pp):
                    xr3 = xr[pp][:, :].rearrange("p (c n) -> p c n", n=512)
                    xk3 = xk[pp][:, :].rearrange("p (c n) -> p c n", n=512)
                    lf = lnfm3s[pp]
                    vector.drain()
                    vector.tensor_tensor(xr3, lf[:, :, 1:513], lf[:, :, 0:512], ALU.subtract)
                    vector.drain()
                    for cc in range(8):
                        instr = vector.scalar_tensor_tensor(
                            xk3[:, cc, :], xr3[:, cc, :], P["muks"][:, cc:cc + 1],
                            lf[:, cc, 0:512], ALU.mult, ALU.add)
                    att(instr)
                stage(tq, "vector", [(TE, TE.n), (SE, SE.n)], [(VE, 1)], v_mix_k)
                mixk_done_arr[tt] = VE.n

            def mix_group_b(tt):
                pp = tt % 2

                def v_mix_r(vector, att, pp=pp):
                    xr3 = xr[pp][:, :].rearrange("p (c n) -> p c n", n=512)
                    lf = lnfm3s[pp]
                    vector.drain()
                    for cc in range(8):
                        vector.scalar_tensor_tensor(
                            xr3[:, cc, :], xr3[:, cc, :], P["murs"][:, cc:cc + 1],
                            lf[:, cc, 0:512], ALU.mult, ALU.add)
                    vector.drain()
                    att(vector.tensor_copy(lnfm3s[1 - pp][:, :, 0:1], lf[:, :, 512:513]))
                stage(tq, "vector", [], [(VE, 1)], v_mix_r)
                mix_done_arr[tt] = VE.n

            def ln2_group(tt):
                def v2_stats(vector, att):
                    vector.drain()
                    for g in range(4):
                        vector.reduce_sum(st2[:, g:g + 1], yt[:, g * 1024:(g + 1) * 1024],
                                          axis=AX.X)
                    vector.drain()
                    att(vector.tensor_scalar(st2[:, 0:4], st2[:, 0:4], 1.0 / D,
                                             None, ALU.mult, ALU.bypass))
                stage(tq, "vector", [(GC, yadd_done_arr[tt])], [(VE, 1)], v2_stats)

                def s2_sq(scalar, att):
                    for g in range(4):
                        scalar.drain()
                        instr = scalar.activation(scrap[:, :], yt[:, g * 1024:(g + 1) * 1024],
                                                  ACT.Square, accum_out=st2[:, 4 + g:5 + g])
                    att(instr)
                stage(tq, "scalar", [(GC, yadd_done_arr[tt])], [(SE, 1)], s2_sq)

                def v2_var(vector, att):
                    vector.drain()
                    vector.tensor_tensor(st2[:, 8:12], st2[:, 0:4], st2[:, 0:4], ALU.mult)
                    vector.drain()
                    att(vector.scalar_tensor_tensor(
                        st2[:, 8:12], st2[:, 4:8], 1.0 / D, st2[:, 8:12],
                        ALU.mult, ALU.subtract))
                stage(tq, "vector", [(SE, SE.n)], [(VE, 1)], v2_var)

                stage(tq, "scalar", [(VE, VE.n)], [(SE, 1)],
                      lambda scalar, att: att(scalar.activation(
                          st2[:, 12:16], st2[:, 8:12], ACT.Sqrt, bias=P["ceps"])))

                def v2_apply(vector, att):
                    vector.drain()
                    vector.reciprocal(st2[:, 12:16], st2[:, 12:16])
                    vector.drain()
                    for g in range(4):
                        instr = vector.tensor_scalar(
                            ztm[:, g * 1024:(g + 1) * 1024],
                            yt[:, g * 1024:(g + 1) * 1024],
                            st2[:, g:g + 1], st2[:, 12 + g:13 + g],
                            ALU.subtract, ALU.mult)
                    att(instr)
                stage(tq, "vector", [(SE, SE.n), (TE, TE.n), (GC, yadd_done_arr[tt])],
                      [(VE, 1)], v2_apply)
                v2a_done[tt] = VE.n

            def ztr_group(tt):
                for g in range(4):
                    ps = pzA if g % 2 == 0 else pzB
                    pname = "pzA" if g % 2 == 0 else "pzB"

                    def trz(tensor, att, g=g, ps=ps):
                        for cc in range(8):
                            instr = tensor.transpose(
                                ps[:, cc * 128:(cc + 1) * 128],
                                ztm[:, g * 1024 + cc * 128:g * 1024 + (cc + 1) * 128],
                                P["identb"][:, :])
                        att(instr)
                    trw = [(VE, v2a_done[tt])] if g == 0 else []
                    stage(tq, "tensor", trw + gw(pname), [(TE, 1)], trz)

                    def evz(scalar, att, g=g, ps=ps):
                        scalar.drain()
                        dst = zfm[:, :].rearrange("p (c n) -> p c n", n=512)[:, :, g * 128:(g + 1) * 128]
                        att(scalar.activation(dst, ps[:, :].rearrange("p (c n) -> p c n", c=8),
                                              ACT.Copy))
                    evw2 = [(TE, TE.n)]
                    if g == 0:
                        evw2.append((ZP, zspill_prev[0]))
                    stage(tq, "scalar", evw2, [(SE, 1)], evz)
                    gset(pname, SE)

                stage(tq, "gpsimd", [(SE, SE.n)], [(ZP, 16)],
                      lambda gp, att, tt=tt: att(gp.dma_start(
                          out=io["z_sp"][:, tt * T1:(tt + 1) * T1].rearrange("(c p) n -> p c n", p=128),
                          in_=zfm[:, :].rearrange("p (c n) -> p c n", c=8))))
                zspill_prev[0] = ZP.n

            ln1_chain(0)
            mix_group(0)
            mix_group_b(0)

            for t in range(NT1):
                par = t % 2

                if t + 1 < NT1:
                    psem = LB if (t + 1) % 2 == 1 else LA
                    pw = [(VE, VE.n), (GC, GC.n)] if t >= 1 else []
                    src = io["u_sp"] if p2b else io["x"]
                    off = (0 if p2b else 128) + (t + 1) * T1
                    stage(tq, "sync", pw, [(psem, 16)],
                          lambda s, att, src=src, off=off, par=par: att(s.dma_start(
                              out=xt[1 - par][:, :].rearrange("p (g n) -> p g n", g=4),
                              in_=src[off:off + T1, :].rearrange("(g p) n -> p g n", p=128))))
                    xt_ready[t + 1] = (psem, psem.n)

                if (not p2b) and t > 0:
                    ln2_group(t - 1)

                for p2 in range(4):
                    half = p2 % 2
                    pname = f"pk{half}"

                    def kbody(tensor, att, p2=p2, half=half, par=par):
                        for o2 in range(2):
                            oc = p2 * 2 + o2
                            for ic in range(8):
                                instr = tensor.matmul(
                                    pk[:, half * 1024 + o2 * 512:half * 1024 + (o2 + 1) * 512],
                                    ffnT["fk"][:, ic * 1024 + oc * 128:ic * 1024 + (oc + 1) * 128],
                                    xk[par][:, ic * 512:(ic + 1) * 512],
                                    start=(ic == 0), stop=(ic == 7))
                        att(instr)
                    stage(tq, "tensor",
                          ([(VE, mixk_done_arr[t])] if p2 == 0 else []) + gw(pname),
                          [(TE, 1)], kbody)

                    def kevac(scalar, att, p2=p2, half=half):
                        sl = slice(p2 * 1024, (p2 + 1) * 1024)
                        psl = pk[:, half * 1024:(half + 1) * 1024]
                        scalar.drain()
                        scalar.activation(kact[:, sl], psl, ACT.Relu)
                        scalar.drain()
                        att(scalar.activation(kact[:, sl], kact[:, sl], ACT.Square))
                    stage(tq, "scalar", [(TE, TE.n)], [(SE, 1)], kevac)
                    gset(pname, SE)
                kact_done = SE.n

                if t + 1 < NT1:
                    ln1_chain(t + 1)

                for tg in range(4):
                    half = tg % 2
                    pname = f"pk{half}"

                    def rbody(tensor, att, tg=tg, half=half, par=par):
                        for j in range(2):
                            for ic in range(8):
                                instr = tensor.matmul(
                                    pk[:, half * 1024 + j * 512:half * 1024 + (j + 1) * 512],
                                    xr[par][:, ic * 512 + tg * 128:ic * 512 + (tg + 1) * 128],
                                    ffnT["fr"][:, ic * 1024 + j * 512:ic * 1024 + (j + 1) * 512],
                                    start=(ic == 0), stop=(ic == 7))
                        att(instr)
                    rbw = [(VE, mix_done_arr[t])] if tg == 0 else []
                    stage(tq, "tensor", rbw + gw(pname), [(TE, 1)], rbody)

                    def tanhb(scalar, att, tg=tg, half=half):
                        scalar.drain()
                        att(scalar.activation(rth[:, tg * 1024:(tg + 1) * 1024],
                                              pk[:, half * 1024:(half + 1) * 1024],
                                              ACT.Tanh, scale=0.5))
                    tanhw = [(TE, TE.n)]
                    if t > 0 and tg == 0:
                        tanhw.append((VE, ybody_last[t - 1]))
                    stage(tq, "scalar", tanhw, [(SE, 1)], tanhb)
                    gset(pname, SE)

                if t + 1 < NT1:
                    mix_group(t + 1)

                for tg in range(4):
                    half = tg % 2
                    pname = f"pk{half}"

                    def kvbody(tensor, att, tg=tg, half=half):
                        for j in range(2):
                            for ic in range(8):
                                instr = tensor.matmul(
                                    pk[:, half * 1024 + j * 512:half * 1024 + (j + 1) * 512],
                                    kact[:, ic * 512 + tg * 128:ic * 512 + (tg + 1) * 128],
                                    ffnT["fv"][:, ic * 1024 + j * 512:ic * 1024 + (j + 1) * 512],
                                    start=(ic == 0), stop=(ic == 7))
                        att(instr)
                    kvw = [(SE, kact_done)] if tg < 2 else []
                    stage(tq, "tensor", kvw + gw(pname), [(TE, 1)], kvbody)

                    def ybody(vector, att, tg=tg, half=half):
                        sl = slice(tg * 1024, (tg + 1) * 1024)
                        vector.drain()
                        att(vector.scalar_tensor_tensor(yt[:, sl], rth[:, sl], 1.0,
                                                        pk[:, half * 1024:(half + 1) * 1024],
                                                        ALU.add, ALU.mult))
                    yw = [(TE, TE.n), (SE, SE.n)]
                    if t > 0 and tg == 0:
                        yw.append((GP, yspill_done[t - 1]))
                    stage(tq, "vector", yw, [(VE, 1)], ybody)
                    gset(pname, VE)

                    def yadd(gpsimd, att, tg=tg, par=par):
                        sl = slice(tg * 1024, (tg + 1) * 1024)
                        att(gpsimd.tensor_tensor(yt[:, sl], yt[:, sl],
                                                 xt[par][:, sl], ALU.add))
                    stage(tq, "gpsimd", [(VE, VE.n), (GC, GC.n)], [(GC, 1)], yadd)
                ybody_last[t] = VE.n
                yadd_done_arr[t] = GC.n
                if t + 1 < NT1:
                    mix_group_b(t + 1)

                if p2b:
                    stage(tq, "gpsimd", [(GC, yadd_done_arr[t])], [(GP, 16)],
                          lambda g, att, t=t: att(g.dma_start(
                              out=io["out"][t * T1:(t + 1) * T1, :].rearrange("(g p) n -> p g n", p=128),
                              in_=yt[:, :].rearrange("p (g n) -> p g n", g=4))))
                else:
                    stage(tq, "gpsimd", [(GC, yadd_done_arr[t])], [(GP, 16)],
                          lambda g, att, t=t: att(g.dma_start(
                              out=io["y_sp"][t * T1:(t + 1) * T1, :].rearrange("(g p) n -> p g n", p=128),
                              in_=yt[:, :].rearrange("p (g n) -> p g n", g=4))))
                yspill_done[t] = GP.n

                if (not p2b) and t > 0:
                    ztr_group(t - 1)
                tq.flush()

            if not p2b:
                ln2_group(NT1 - 1)
                ztr_group(NT1 - 1)
                tq.flush()

            @blk.gpsimd
            def _(gpsimd):
                wt(gpsimd, GP)
                if not p2b:
                    wt(gpsimd, ZP)


def phase_p1b(nc, io, P, S):
    LD, TE, VE, SE, GP = (S[k] for k in ["LD", "TE", "VE", "SE", "GP"])
    LA, LB = S["LA"], S["LB"]

    with ExitStack() as ctx:
        e = ctx.enter_context
        wqb = e(nc.sbuf_tensor("wqb", [128, 8 * D], BF16))
        wkb = e(nc.sbuf_tensor("wkb", [128, 8 * D], BF16))
        wvb = e(nc.sbuf_tensor("wvb", [128, 8 * DF], BF16))
        gw1b = e(nc.sbuf_tensor("gw1b", [128, 8 * 16], BF16))
        gw2b = e(nc.sbuf_tensor("gw2b", [16, D], BF16))
        Hf = e(nc.sbuf_tensor("Hf", [128, 8 * DV], F32))
        Hbf = e(nc.sbuf_tensor("Hbf", [128, 8 * DV], BF16))
        zt = [e(nc.sbuf_tensor(f"zt{i}", [128, 8 * T2], BF16)) for i in range(2)]
        gk1sb = e(nc.sbuf_tensor("gk1sb", [16, T2], BF16))
        nbc = [e(nc.sbuf_tensor(f"nbc{i}", [128, 8 * T2], F32)) for i in range(2)]
        e1b = [e(nc.sbuf_tensor(f"e1b{i}", [128, 8 * T2], F32)) for i in range(2)]
        expb = e(nc.sbuf_tensor("expb", [128, 8 * T2], F32))
        expnb = e(nc.sbuf_tensor("expnb", [128, 8 * T2], F32))
        ebl8 = [e(nc.sbuf_tensor(f"ebl8_{i}", [128, 8], F32)) for i in range(2)]
        qg = e(nc.sbuf_tensor("qg", [128, 8 * T2], BF16))
        kg = e(nc.sbuf_tensor("kg", [128, 8 * T2], BF16))
        kH = e(nc.sbuf_tensor("kH", [128, 8 * T2], BF16))
        kHtm = e(nc.sbuf_tensor("kHtm", [128, 2 * D], BF16))
        vtm = e(nc.sbuf_tensor("vtm", [128, 2 * DF], BF16))
        Asb = e(nc.sbuf_tensor("Asb", [128, 8 * T2], BF16))
        osb = e(nc.sbuf_tensor("osb", [128, 16 * T2], BF16))
        caus2 = e(nc.sbuf_tensor("caus2_1b", [128, 4 * C], BF16))
        b2row = e(nc.sbuf_tensor("b2row_1b", [1, D], F32))
        pv = e(nc.psum_tensor("pv_1b", [128, 2048], F32))
        po = e(nc.psum_tensor("po_1b", [128, 1024], F32))
        pb = e(nc.psum_tensor("pb_1b", [128, 1024], BF16))
        pg = e(nc.psum_tensor("pg_1b", [128, 512], F32))

        guards = {}

        def gw(name):
            g = guards.get(name)
            return [g] if g is not None else []

        def gset(name, sem):
            guards[name] = (sem, sem.n)

        with nc.Block() as blk:
            tq = TQ(blk)

            W1, W2, W3 = S["W1"], S["W2"], S["W3"]

            @blk.gpsimd
            def _(gpsimd):
                GP.inc(gpsimd.dma_start(
                    out=gw1b[:, :].rearrange("p (c n) -> p c n", c=8),
                    in_=io["gw1"][:, :].rearrange("(c p) n -> p c n", p=128)), 16)
                GP.inc(gpsimd.dma_start(out=gw2b[:, :], in_=io["gw2"][:, :]), 16)
                GP.inc(gpsimd.dma_start(out=b2row[:, :], in_=io["gb2"][:, :]), 16)
                GP.inc(gpsimd.dma_start(out=caus2[:, 0:2 * C], in_=io["caus"][:, :]), 16)
                GP.inc(gpsimd.dma_start(out=caus2[:, 2 * C:4 * C], in_=io["caus"][:, :]), 16)
                for w, dst, sem in [("wq", wqb, W1), ("wk", wkb, W2), ("wv", wvb, W3)]:
                    sem.inc(gpsimd.dma_start(
                        out=dst[:, :].rearrange("p (c n) -> p c n", c=8),
                        in_=io[w][:, :].rearrange("(c p) n -> p c n", p=128)), 16)
            wq_ld, wk_ld, wv_ld = W1.n, W2.n, W3.n

            @blk.vector
            def _(vector):
                vector.memset(Hf[:, :], 0.0)
                VE.inc(vector.memset(Hbf[:, :], 0.0))

            zt_ready = [None] * NT2

            @blk.sync
            def _(sync):
                zt_ready[0] = (LA, LA.inc(sync.dma_start(
                    out=zt[0][:, :].rearrange("p (c n) -> p c n", c=8),
                    in_=io["z_sp"][:, 0:T2].rearrange("(c p) n -> p c n", p=128)), 16))

            spill_done = GP.n
            w_ready = GP.n

            def gk_chain(tt):
                pp = tt % 2

                def gk1b(tensor, att, pp=pp):
                    for ic in range(8):
                        instr = tensor.matmul(
                            pg[0:16, 0:T2], gw1b[:, ic * 16:(ic + 1) * 16],
                            zt[pp][:, ic * T2:(ic + 1) * T2],
                            start=(ic == 0), stop=(ic == 7))
                    att(instr)
                gkw = [zt_ready[tt]] + gw("pg")
                if tt == 0:
                    gkw.append((GP, w_ready))
                stage(tq, "tensor", gkw, [(TE, 1)], gk1b)

                def gk1ev(vector, att):
                    vector.drain()
                    att(vector.tensor_copy(gk1sb[:, :], pg[0:16, 0:T2]))
                stage(tq, "vector", [(TE, TE.n)], [(VE, 1)], gk1ev)
                gset("pg", VE)

                for rr in range(4):
                    def gk2b(tensor, att, rr=rr):
                        for o2 in range(2):
                            oc = rr * 2 + o2
                            tensor.matmul(
                                pg[:, o2 * T2:(o2 + 1) * T2],
                                gw2b[:, oc * 128:(oc + 1) * 128], gk1sb[:, :],
                                start=True, stop=False)
                            instr = tensor.matmul(
                                pg[:, o2 * T2:(o2 + 1) * T2],
                                b2row[0:1, oc * 128:(oc + 1) * 128],
                                P["ones1"][0:1, 0:T2],
                                start=False, stop=True)
                        att(instr)
                    stage(tq, "tensor", ([(VE, VE.n)] if rr == 0 else []) + gw("pg"),
                          [(TE, 1)], gk2b)

                    def gkev(scalar, att, rr=rr, pp=pp):
                        sl = slice(rr * 512, (rr + 1) * 512)
                        scalar.drain()
                        scalar.activation(e1b[pp][:, sl], pg[:, :], ACT.Exp, scale=-1.0)
                        scalar.drain()
                        att(scalar.activation(e1b[pp][:, sl], e1b[pp][:, sl],
                                              ACT.Ln, bias=P["cone"]))
                    stage(tq, "scalar", [(TE, TE.n), (VE, VE.n)], [(SE, 1)], gkev)
                    gset("pg", SE)

                    def scn(vector, att, rr=rr, pp=pp, tt=tt):
                        vector.drain()
                        for o2 in range(2):
                            oc = rr * 2 + o2
                            instr = vector.tensor_tensor_scan(
                                nbc[pp][:, oc * T2:(oc + 1) * T2],
                                e1b[pp][:, oc * T2:(oc + 1) * T2],
                                e1b[pp][:, oc * T2:(oc + 1) * T2],
                                0.0, ALU.add, ALU.bypass)
                        if rr == 3:
                            vector.drain()
                            bl3 = P["BL"][:, :].rearrange("p (c n) -> p c n", n=16)
                            nbc3 = nbc[pp][:, :].rearrange("p (c n) -> p c n", n=T2)
                            instr = vector.tensor_copy(bl3[:, :, tt:tt + 1],
                                                       nbc3[:, :, T2 - 1:T2])
                        att(instr)
                    stage(tq, "vector", [(SE, SE.n)], [(VE, 1)], scn)

            def exps_stage(tt):
                pp = tt % 2

                def exps(scalar, att, tt=tt, pp=pp):
                    bl3 = P["BL"][:, :].rearrange("p (c n) -> p c n", n=16)
                    scalar.drain()
                    scalar.activation(ebl8[pp][:, :].rearrange("p (c n) -> p c n", n=1),
                                      bl3[:, :, tt:tt + 1], ACT.Exp, scale=-1.0 / 16.0)
                    scalar.activation(expb[:, :], nbc[pp][:, :], ACT.Exp,
                                      scale=-1.0 / 16.0, bias=P["cln16n"])
                    att(scalar.activation(expnb[:, :], nbc[pp][:, :], ACT.Exp,
                                          scale=1.0 / 16.0))
                stage(tq, "scalar", [(VE, VE.n)], [(SE, 1)], exps)

            gk_chain(0)
            exps_stage(0)

            for t in range(NT2):
                par = t % 2

                if t + 1 < NT2:
                    psem = LB if (t + 1) % 2 == 1 else LA
                    pw = [(TE, TE.n)] if t >= 1 else []
                    stage(tq, "sync", pw, [(psem, 16)],
                          lambda s, att, t=t, par=par: att(s.dma_start(
                              out=zt[1 - par][:, :].rearrange("p (c n) -> p c n", c=8),
                              in_=io["z_sp"][:, (t + 1) * T2:(t + 2) * T2].rearrange("(c p) n -> p c n", p=128))))
                    zt_ready[t + 1] = (psem, psem.n)

                for p2 in range(4):
                    def qb(tensor, att, p2=p2, par=par):
                        for o2 in range(2):
                            oc = p2 * 2 + o2
                            for ic in range(8):
                                instr = tensor.matmul(
                                    pv[:, oc * T2:(oc + 1) * T2],
                                    wqb[:, ic * 1024 + oc * 128:ic * 1024 + (oc + 1) * 128],
                                    zt[par][:, ic * T2:(ic + 1) * T2],
                                    start=(ic == 0), stop=(ic == 7))
                        att(instr)
                    qbw = [(W1, wq_ld)] if t == 0 and p2 == 0 else []
                    stage(tq, "tensor", qbw + gw(f"pv{p2}"), [(TE, 1)], qb)

                    def qev(vector, att, p2=p2, t=t):
                        vector.drain()
                        att(vector.tensor_tensor(qg[:, p2 * 512:(p2 + 1) * 512],
                                                 pv[:, p2 * 512:(p2 + 1) * 512],
                                                 expb[:, p2 * 512:(p2 + 1) * 512],
                                                 ALU.mult))
                    qw = [(TE, TE.n), (SE, SE.n)]
                    if t > 0 and p2 == 0:
                        qw.append((GP, spill_done))
                    stage(tq, "vector", qw, [(VE, 1)], qev)
                    gset(f"pv{p2}", VE)

                stage(tq, "gpsimd", [(VE, VE.n)], [(GP, 16)],
                      lambda g, att, t=t: att(g.dma_start(
                          out=io["qg_sp"][:, t * T2:(t + 1) * T2].rearrange("(c p) n -> p c n", p=128),
                          in_=qg[:, :].rearrange("p (c n) -> p c n", c=8))))

                for p2 in range(4):
                    def kb(tensor, att, p2=p2, par=par):
                        for o2 in range(2):
                            oc = p2 * 2 + o2
                            for ic in range(8):
                                instr = tensor.matmul(
                                    pv[:, oc * T2:(oc + 1) * T2],
                                    wkb[:, ic * 1024 + oc * 128:ic * 1024 + (oc + 1) * 128],
                                    zt[par][:, ic * T2:(ic + 1) * T2],
                                    start=(ic == 0), stop=(ic == 7))
                        att(instr)
                    kbw = [(W2, wk_ld)] if t == 0 and p2 == 0 else []
                    stage(tq, "tensor", kbw + gw(f"pv{p2}"), [(TE, 1)], kb)

                    def kev(vector, att, p2=p2, par=par):
                        sl = slice(p2 * 512, (p2 + 1) * 512)
                        vector.drain()
                        vector.tensor_tensor(kg[:, sl], pv[:, sl], expnb[:, sl], ALU.mult)
                        vector.drain()
                        for o2 in range(2):
                            oc = p2 * 2 + o2
                            instr = vector.tensor_scalar(kH[:, oc * T2:(oc + 1) * T2],
                                                         kg[:, oc * T2:(oc + 1) * T2],
                                                         ebl8[par][:, oc:oc + 1], None,
                                                         ALU.mult, ALU.bypass)
                        att(instr)
                    stage(tq, "vector", [(TE, TE.n)], [(VE, 1)], kev)
                    gset(f"pv{p2}", VE)
                kg_done = VE.n

                for tg in range(2):
                    def ktr(tensor, att, tg=tg):
                        for dc in range(8):
                            instr = tensor.transpose(
                                pb[:, dc * 128:(dc + 1) * 128],
                                kH[:, dc * T2 + tg * 128:dc * T2 + (tg + 1) * 128],
                                P["identb"][:, :])
                        att(instr)
                    kw = [(VE, kg_done)] if tg == 0 else []
                    stage(tq, "tensor", kw + gw("pb"), [(TE, 1)], ktr)

                    def ktrev(vector, att, tg=tg):
                        vector.drain()
                        att(vector.tensor_copy(kHtm[:, tg * 1024:(tg + 1) * 1024], pb[:, :]))
                    stage(tq, "vector", [(TE, TE.n)], [(VE, 1)], ktrev)
                    gset("pb", VE)

                if t + 1 < NT2:
                    gk_chain(t + 1)

                for tg in range(2):
                    for j2 in range(2):
                        def vb(tensor, att, tg=tg, j2=j2, par=par):
                            for j in range(j2 * 2, j2 * 2 + 2):
                                for ic in range(8):
                                    instr = tensor.matmul(
                                        pv[:, j * 512:(j + 1) * 512],
                                        zt[par][:, ic * T2 + tg * 128:ic * T2 + (tg + 1) * 128],
                                        wvb[:, ic * 2048 + j * 512:ic * 2048 + (j + 1) * 512],
                                        start=(ic == 0), stop=(ic == 7))
                            att(instr)
                        vbw = [(W3, wv_ld)] if t == 0 and tg == 0 and j2 == 0 else []
                        stage(tq, "tensor", vbw + gw(f"pv{j2 * 2}") + gw(f"pv{j2 * 2 + 1}"),
                              [(TE, 1)], vb)

                        def vev(vector, att, tg=tg, j2=j2):
                            vector.drain()
                            att(vector.tensor_copy(
                                vtm[:, tg * 2048 + j2 * 1024:tg * 2048 + (j2 + 1) * 1024],
                                pv[:, j2 * 1024:(j2 + 1) * 1024]))
                        stage(tq, "vector", [(TE, TE.n)], [(VE, 1)], vev)
                        gset(f"pv{j2 * 2}", VE)
                        gset(f"pv{j2 * 2 + 1}", VE)
                vtm_done = VE.n

                for hp in range(2):
                    def ab(tensor, att, hp=hp):
                        for hh in range(2):
                            h = hp * 2 + hh
                            for sg in range(2):
                                for dc2 in range(2):
                                    oc = 2 * h + dc2
                                    instr = tensor.matmul(
                                        po[:, (hh * 2 + sg) * T2:(hh * 2 + sg + 1) * T2],
                                        kg[:, oc * T2 + sg * 128:oc * T2 + (sg + 1) * 128],
                                        qg[:, oc * T2:(oc + 1) * T2],
                                        start=(dc2 == 0), stop=(dc2 == 1))
                        att(instr)
                    stage(tq, "tensor", [(VE, kg_done)] + gw("po"), [(TE, 1)], ab)

                    def aev(vector, att, hp=hp):
                        vector.drain()
                        att(vector.tensor_tensor(
                            Asb[:, hp * 4 * T2:(hp + 1) * 4 * T2],
                            po[:, :], caus2[:, :], ALU.mult))
                    stage(tq, "vector", [(TE, TE.n)], [(VE, 1)], aev)
                    gset("po", VE)
                    aev_done = VE.n

                    for hh in range(2):
                        h = hp * 2 + hh

                        def ob(tensor, att, h=h, hh=hh):
                            for ec in range(4):
                                col = (hh * 4 + ec) * T2
                                first = True
                                for sg in range(2):
                                    tensor.matmul(
                                        pv[:, col:col + T2],
                                        vtm[:, sg * 2048 + h * 512 + ec * 128:sg * 2048 + h * 512 + (ec + 1) * 128],
                                        Asb[:, (h * 2 + sg) * T2:(h * 2 + sg + 1) * T2],
                                        start=first, stop=False)
                                    first = False
                                for dc2 in range(2):
                                    oc = 2 * h + dc2
                                    instr = tensor.matmul(
                                        pv[:, col:col + T2],
                                        Hbf[:, oc * 512 + ec * 128:oc * 512 + (ec + 1) * 128],
                                        qg[:, oc * T2:(oc + 1) * T2],
                                        start=False, stop=(dc2 == 1))
                            att(instr)
                        stage(tq, "tensor",
                              [(VE, aev_done)] + gw(f"pv{hh * 2}") + gw(f"pv{hh * 2 + 1}"),
                              [(TE, 1)], ob)

                        def oev(scalar, att, h=h, hh=hh, t=t):
                            scalar.drain()
                            att(scalar.activation(osb[:, h * 4 * T2:(h + 1) * 4 * T2],
                                                  pv[:, hh * 1024:(hh + 1) * 1024], ACT.Copy))
                        ow = [(TE, TE.n)]
                        if hp == 0 and hh == 0 and t > 0:
                            ow.append((GP, spill_done))
                        stage(tq, "scalar", ow, [(SE, 1)], oev)
                        gset(f"pv{hh * 2}", SE)
                        gset(f"pv{hh * 2 + 1}", SE)

                stage(tq, "gpsimd", [(SE, SE.n)], [(GP, 16)],
                      lambda g, att, t=t: att(g.dma_start(
                          out=io["o_sp"][:, t * T2:(t + 1) * T2].rearrange("(c p) n -> p c n", p=128),
                          in_=osb[:, :].rearrange("p (c n) -> p c n", c=16))))
                spill_done = GP.n

                if t + 1 < NT2:
                    exps_stage(t + 1)

                for hp in range(2):
                    def ubm(tensor, att, hp=hp):
                        for hh in range(2):
                            h = hp * 2 + hh
                            for dc2 in range(2):
                                oc = 2 * h + dc2
                                for sg in range(2):
                                    instr = tensor.matmul(
                                        pv[:, (hh * 2 + dc2) * 512:(hh * 2 + dc2 + 1) * 512],
                                        kHtm[:, sg * 1024 + oc * 128:sg * 1024 + (oc + 1) * 128],
                                        vtm[:, sg * 2048 + h * 512:sg * 2048 + (h + 1) * 512],
                                        start=(sg == 0), stop=(sg == 1))
                        att(instr)
                    stage(tq, "tensor",
                          [(VE, vtm_done)] + gw("pv0") + gw("pv1") + gw("pv2") + gw("pv3"),
                          [(TE, 1)], ubm)

                    def hup(vector, att, hp=hp, par=par):
                        vector.drain()
                        for hh in range(2):
                            h = hp * 2 + hh
                            for dc2 in range(2):
                                oc = 2 * h + dc2
                                instr = vector.scalar_tensor_tensor(
                                    Hf[:, oc * 512:(oc + 1) * 512],
                                    Hf[:, oc * 512:(oc + 1) * 512],
                                    ebl8[par][:, oc:oc + 1],
                                    pv[:, (hh * 2 + dc2) * 512:(hh * 2 + dc2 + 1) * 512],
                                    ALU.mult, ALU.add)
                        if hp == 1:
                            vector.drain()
                            instr = vector.tensor_copy(Hbf[:, :], Hf[:, :])
                        att(instr)
                    stage(tq, "vector", [(TE, TE.n)], [(VE, 1)], hup)
                    for p2 in range(4):
                        gset(f"pv{p2}", VE)

                tq.flush()

            @blk.vector
            def _(vector):
                vector.drain()
                vector.tensor_scalar(Hf[:, :], Hf[:, :], P["role"][:, :],
                                     None, ALU.mult, ALU.bypass)
                vector.drain()
                VE.inc(vector.tensor_copy(Hbf[:, :], Hf[:, :]))

            @blk.gpsimd
            def _(gpsimd):
                wt(gpsimd, VE)
                GP.inc(gpsimd.dma_start(out=io["st_snd"][:, :], in_=Hbf[:, :]), 16)
                GP.inc(gpsimd.dma_start(out=io["bl_sp"][:, :], in_=P["BL"][:, :]), 16)
                wt(gpsimd, GP)


def phase_p2a(nc, io, P, S, dev):
    LD, TE, VE, SE, GP, CC = (S[k] for k in ["LD", "TE", "VE", "SE", "GP", "CC"])
    LA, LB = S["LA"], S["LB"]

    with ExitStack() as ctx:
        e = ctx.enter_context
        wgb = e(nc.sbuf_tensor("wgb", [128, 8 * DF], BF16))
        wob = e(nc.sbuf_tensor("wob", [128, 16 * D], BF16))
        Spb = e(nc.sbuf_tensor("Spb", [128, 8 * DV], BF16))
        EXPOFF = e(nc.sbuf_tensor("EXPOFF", [128, 128], F32))
        offt = e(nc.sbuf_tensor("offt", [128, 16], F32))
        zt = [e(nc.sbuf_tensor(f"z2_{i}", [128, 8 * T2], BF16)) for i in range(2)]
        qgt = [e(nc.sbuf_tensor(f"qg2_{i}", [128, 8 * T2], BF16)) for i in range(2)]
        ot = [e(nc.sbuf_tensor(f"o2_{i}", [128, 16 * T2], BF16)) for i in range(2)]
        yt = e(nc.sbuf_tensor("y2", [128, 2 * D], F32))
        qgD = e(nc.sbuf_tensor("qgD", [128, 8 * T2], BF16))
        ofl = [e(nc.sbuf_tensor(f"ofl{i}", [128, 16 * T2], F32)) for i in range(2)]
        osq = e(nc.sbuf_tensor("osq", [128, 16 * T2], BF16))
        sg = [e(nc.sbuf_tensor(f"sg2_{i}", [128, 16 * T2], BF16)) for i in range(2)]
        ogb = [e(nc.sbuf_tensor(f"ogb{i}", [128, 16 * T2], BF16)) for i in range(2)]
        ut = e(nc.sbuf_tensor("ut", [128, 2 * D], F32))
        srow = [e(nc.sbuf_tensor(f"srow{i}", [1, 4 * T2], F32)) for i in range(2)]
        ubrow = srow[0]
        st3 = e(nc.sbuf_tensor("st3", [128, 8], F32))
        pv = e(nc.psum_tensor("pv_2a", [128, 2048], F32))
        po = e(nc.psum_tensor("po_2a", [128, 1024], F32))
        ps = e(nc.psum_tensor("ps_2a", [128, 1024], F32))

        guards = {}

        def gw(name):
            g = guards.get(name)
            return [g] if g is not None else []

        def gset(name, sem):
            guards[name] = (sem, sem.n)

        with nc.Block() as blk:
            tq = TQ(blk)

            WB, WC = S["WB"], S["WC"]

            @blk.gpsimd
            def _(gpsimd):
                GP.inc(gpsimd.dma_start(
                    out=wgb[:, :].rearrange("p (c n) -> p c n", c=8)[:, :, 0:1024],
                    in_=io["wg"][:, 0:1024].rearrange("(c p) n -> p c n", p=128)), 16)
                WB.inc(gpsimd.dma_start(
                    out=wgb[:, :].rearrange("p (c n) -> p c n", c=8)[:, :, 1024:2048],
                    in_=io["wg"][:, 1024:2048].rearrange("(c p) n -> p c n", p=128)), 16)
                WC.inc(gpsimd.dma_start(
                    out=wob[:, :].rearrange("p (c n) -> p c n", c=16),
                    in_=io["wo"][:, :].rearrange("(c p) n -> p c n", p=128)), 16)
                if not dev:
                    CC.inc(gpsimd.collective_compute(
                        "AllReduce", ALU.add, replica_groups=PAIRS,
                        ins=[io["st_snd"].opt()], outs=[io["st_rcv"].opt()]))

            if dev:
                @blk.sync
                def _(sync):
                    LD.inc(sync.dma_start(out=P["BL"][:, :], in_=io["bl_sp"][:, :]), 16)

            wgb_done = GP.n
            wgb1_done = WB.n

            @blk.vector
            def _(vector):
                wt(vector, WC)
                for fc in range(16):
                    instr = vector.tensor_scalar(
                        wob[:, fc * D:(fc + 1) * D], wob[:, fc * D:(fc + 1) * D],
                        P["gnwc"][:, fc % 4:fc % 4 + 1], None, ALU.mult, ALU.bypass)
                VE.inc(instr)
            w2a_done = VE.n

            @blk.vector
            def _(vector):
                if dev:
                    wt(vector, LD)
                vector.drain()
                for dc in range(8):
                    vector.tensor_tensor_scan(
                        offt[:, :], P["BL"][:, dc * 16:(dc + 1) * 16],
                        P["BL"][:, dc * 16:(dc + 1) * 16], 0.0, ALU.add, ALU.bypass)
                    vector.drain()
                    instr = vector.tensor_tensor(
                        EXPOFF[:, dc * 16:(dc + 1) * 16], offt[:, :],
                        P["BL"][:, dc * 16:(dc + 1) * 16], ALU.subtract)
                    vector.drain()
                VE.inc(instr)

            @blk.scalar
            def _(scalar):
                wt(scalar, VE)
                scalar.drain()
                SE.inc(scalar.activation(EXPOFF[:, :], EXPOFF[:, :], ACT.Exp,
                                         scale=-1.0 / 16.0))
            expoff_done = SE.n

            in_ready = [None] * NT2
            y_ready = [None] * NT2
            spill_done = GP.n

            order = [NT2 - 1] + list(range(NT2 - 1))

            @blk.sync
            def _(sync):
                s0 = order[0] * T2
                for dst, srcn, cd in [(zt[0], "z_sp", 8), (qgt[0], "qg_sp", 8),
                                      (ot[0], "o_sp", 16)]:
                    sync.dma_start(out=dst[:, :].rearrange("p (c n) -> p c n", c=cd),
                                   in_=io[srcn][:, s0:s0 + T2].rearrange("(c p) n -> p c n", p=128)).then_inc(LA.h, 16)
                sync.dma_start(out=yt[:, :].rearrange("p (g n) -> p g n", g=2),
                               in_=io["y_sp"][s0:s0 + T2, :].rearrange("(g p) n -> p g n", p=128)).then_inc(LA.h, 16)
                LA.n += 64
                in_ready[0] = (LA, LA.n)

            @blk.sync
            def _(sync):
                s0 = order[1] * T2
                for dst, srcn, cd in [(zt[1], "z_sp", 8), (qgt[1], "qg_sp", 8),
                                      (ot[1], "o_sp", 16)]:
                    sync.dma_start(out=dst[:, :].rearrange("p (c n) -> p c n", c=cd),
                                   in_=io[srcn][:, s0:s0 + T2].rearrange("(c p) n -> p c n", p=128)).then_inc(LB.h, 16)
                LB.n += 48
                in_ready[1] = (LB, LB.n)

            SP = S["SP"]

            @blk.gpsimd
            def _(gpsimd):
                if not dev:
                    wt(gpsimd, CC)
                SP.inc(gpsimd.dma_start(out=Spb[:, :], in_=io["st_rcv"][:, :]), 16)
            st_ld = SP.n

            @blk.vector
            def _(vector):
                wt(vector, SP, st_ld)
                vector.drain()
                VE.inc(vector.tensor_scalar(Spb[:, :], Spb[:, :], P["use"][:, :], None,
                                            ALU.mult, ALU.bypass))
            spb_done = VE.n

            qgd_done_arr = [None] * NT2
            silu3_arr = [None] * NT2
            cev3_arr = [None] * NT2
            cb_last_arr = [None] * NT2
            osq_done_arr = [None] * NT2
            inv_done_arr = [None] * NT2
            ub_done_arr = [None] * NT2
            ubsnd_holder = [None]
            ones_after_arr = [None] * NT2
            wo1_arr = [None] * NT2

            def inp_stage(tt):
                parw = tt % 2
                tid = order[tt]

                def qgd(vector, att, tid=tid, parw=parw):
                    vector.drain()
                    for oc in range(8):
                        instr = vector.tensor_scalar(
                            qgD[:, oc * T2:(oc + 1) * T2], qgt[parw][:, oc * T2:(oc + 1) * T2],
                            EXPOFF[:, oc * 16 + tid:oc * 16 + tid + 1], None,
                            ALU.mult, ALU.bypass)
                    att(instr)
                qw = [in_ready[tt], (SE, expoff_done)]
                if tt >= 1:
                    qw.append((TE, cb_last_arr[tt - 1]))
                stage(tq, "vector", qw, [(VE, 1)], qgd)
                qgd_done_arr[tt] = VE.n

                for q in range(4):
                    half = q % 2
                    pn0, pn1 = f"pv{half * 2}", f"pv{half * 2 + 1}"

                    def gb(tensor, att, q=q, half=half, parw=parw):
                        for f4 in range(4):
                            fc = q * 4 + f4
                            col = half * 1024 + f4 * T2
                            for ic in range(8):
                                instr = tensor.matmul(
                                    pv[:, col:col + T2],
                                    wgb[:, ic * DF + fc * 128:ic * DF + (fc + 1) * 128],
                                    zt[parw][:, ic * T2:(ic + 1) * T2],
                                    start=(ic == 0), stop=(ic == 7))
                        att(instr)
                    gbw = [in_ready[tt]] if q < 2 else []
                    if tt == 0 and q < 2:
                        gbw.append((GP, wgb_done))
                    if tt == 0 and q == 2:
                        gbw.append((WB, wgb1_done))
                    stage(tq, "tensor", gbw + gw(pn0) + gw(pn1), [(TE, 1)], gb)

                    def silub(scalar, att, q=q, half=half, parw=parw):
                        scalar.drain()
                        att(scalar.activation(sg[parw][:, q * 4 * T2:(q + 1) * 4 * T2],
                                              pv[:, half * 1024:(half + 1) * 1024],
                                              ACT.Silu))
                    stage(tq, "scalar", [(TE, TE.n)], [(SE, 1)], silub)
                    gset(pn0, SE)
                    gset(pn1, SE)
                silu3_arr[tt] = SE.n

                for u in range(4):
                    h = u
                    half = u % 2
                    pn0, pn1 = f"pv{half * 2}", f"pv{half * 2 + 1}"

                    def cb(tensor, att, h=h, half=half):
                        for ec in range(4):
                            col = half * 1024 + ec * T2
                            for dc2 in range(2):
                                oc = 2 * h + dc2
                                instr = tensor.matmul(
                                    pv[:, col:col + T2],
                                    Spb[:, oc * DV + ec * 128:oc * DV + (ec + 1) * 128],
                                    qgD[:, oc * T2:(oc + 1) * T2],
                                    start=(dc2 == 0), stop=(dc2 == 1))
                        att(instr)
                    cw = [(VE, qgd_done_arr[tt])] if u < 2 else []
                    if tt == 0 and u < 2:
                        cw.append((VE, spb_done))
                    stage(tq, "tensor", cw + gw(pn0) + gw(pn1), [(TE, 1)], cb)

                    def cev(vector, att, h=h, half=half, parw=parw):
                        vector.drain()
                        att(vector.tensor_tensor(
                            ofl[parw][:, h * 4 * T2:(h + 1) * 4 * T2],
                            pv[:, half * 1024:(half + 1) * 1024],
                            ot[parw][:, h * 4 * T2:(h + 1) * 4 * T2], ALU.add))
                    stage(tq, "vector", [(TE, TE.n)], [(VE, 1)], cev)
                    gset(pn0, VE)
                    gset(pn1, VE)
                cev3_arr[tt] = VE.n
                cb_last_arr[tt] = TE.n

            def apply_stage(tt):
                parw = tt % 2
                tid = order[tt]

                def bcb(tensor, att, tt=tt, parw=parw):
                    for h in range(4):
                        instr = tensor.matmul(
                            ps[:, h * T2:(h + 1) * T2],
                            P["ones1"][0:1, 0:128], srow[parw][:, h * T2:(h + 1) * T2],
                            start=True, stop=True)
                    att(instr)
                stage(tq, "tensor", [(SE, inv_done_arr[tt])] + gw("ps"), [(TE, 1)], bcb)
                bcast_done = TE.n

                def onb(vector, att, parw=parw):
                    instr = None
                    for h in range(4):
                        vector.drain()
                        for ec in range(4):
                            idx = (h * 4 + ec) * T2
                            instr = vector.tensor_tensor(ogb[parw][:, idx:idx + T2],
                                                         ogb[parw][:, idx:idx + T2],
                                                         ps[:, h * T2:(h + 1) * T2], ALU.mult)
                    att(instr)
                stage(tq, "vector", [(TE, bcast_done)], [(VE, 1)], onb)
                og_done = VE.n
                gset("ps", VE)

                def wo0(tensor, att, parw=parw):
                    for j in range(2):
                        for fc in range(16):
                            instr = tensor.matmul(
                                po[:, j * 512:(j + 1) * 512],
                                ogb[parw][:, fc * T2 + 0 * 128:fc * T2 + 1 * 128],
                                wob[:, fc * D + j * 512:fc * D + (j + 1) * 512],
                                start=(fc == 0), stop=(fc == 15))
                    att(instr)
                wow = [(VE, og_done)] + gw("po")
                if tt == 0:
                    wow.append((VE, w2a_done))
                stage(tq, "tensor", wow, [(TE, 1)], wo0)

                def wo1(tensor, att, parw=parw):
                    for j in range(2):
                        for fc in range(16):
                            instr = tensor.matmul(
                                pv[:, j * 512:(j + 1) * 512],
                                ogb[parw][:, fc * T2 + 1 * 128:fc * T2 + 2 * 128],
                                wob[:, fc * D + j * 512:fc * D + (j + 1) * 512],
                                start=(fc == 0), stop=(fc == 15))
                    att(instr)
                stage(tq, "tensor", gw("pv0") + gw("pv1"), [(TE, 1)], wo1)
                wo1_done = TE.n
                wo1_arr[tt] = TE.n

                def ub_(vector, att):
                    vector.drain()
                    vector.tensor_tensor(ut[:, 0:D], po[:, 0:D], yt[:, 0:D], ALU.add)
                    att(vector.tensor_tensor(ut[:, D:2 * D], pv[:, 0:D],
                                             yt[:, D:2 * D], ALU.add))
                uw = [(TE, wo1_done)]
                if tt > 0:
                    uw.append(y_ready[tt])
                    uw.append((GP, spill_done))
                stage(tq, "vector", uw, [(VE, 1)], ub_)
                gset("po", VE)
                gset("pv0", VE)
                gset("pv1", VE)
                ub_done_arr[tt] = VE.n

                def l3red(vector, att):
                    vector.drain()
                    for tg in range(2):
                        vector.reduce_sum(st3[:, tg:tg + 1], ut[:, tg * D:(tg + 1) * D],
                                          axis=AX.X)
                    vector.drain()
                    att(vector.tensor_scalar(st3[:, 0:2], st3[:, 0:2], 1.0 / D, None,
                                             ALU.mult, ALU.bypass))
                stage(tq, "vector", [], [(VE, 1)], l3red)

                def l3sq(scalar, att):
                    for tg in range(2):
                        scalar.drain()
                        instr = scalar.activation(osq[:, 0:1024],
                                                  ut[:, tg * D:(tg + 1) * D],
                                                  ACT.Square, accum_out=st3[:, 2 + tg:3 + tg])
                    att(instr)
                onesg = ones_after_arr[tt + 1 if tt + 1 < NT2 else tt]
                stage(tq, "scalar", [(VE, VE.n), (TE, onesg)], [(SE, 1)], l3sq)

                def l3var(vector, att):
                    vector.drain()
                    vector.tensor_tensor(st3[:, 4:6], st3[:, 0:2], st3[:, 0:2], ALU.mult)
                    vector.drain()
                    att(vector.scalar_tensor_tensor(
                        st3[:, 4:6], st3[:, 2:4], 1.0 / D, st3[:, 4:6],
                        ALU.mult, ALU.subtract))
                stage(tq, "vector", [(SE, SE.n)], [(VE, 1)], l3var)

                stage(tq, "scalar", [(VE, VE.n)], [(SE, 1)],
                      lambda scalar, att: (scalar.drain(),
                                           att(scalar.activation(st3[:, 6:8], st3[:, 4:6],
                                                                 ACT.Sqrt, bias=P["ceps"]))))

                def l3fin(vector, att, tid=tid):
                    vector.drain()
                    vector.reciprocal(st3[:, 6:8], st3[:, 6:8])
                    vector.drain()
                    vector.tensor_copy(P["l3m"][:, 2 * tid:2 * tid + 2], st3[:, 0:2])
                    att(vector.tensor_copy(P["l3i"][:, 2 * tid:2 * tid + 2], st3[:, 6:8]))
                stage(tq, "vector", [(SE, SE.n)], [(VE, 1)], l3fin)

                stage(tq, "gpsimd", [(VE, VE.n)], [(GP, 16)],
                      lambda g, att, tid=tid: att(g.dma_start(
                          out=io["u_sp"][tid * T2:(tid + 1) * T2, :].rearrange("(g p) n -> p g n", p=128),
                          in_=ut[:, :].rearrange("p (g n) -> p g n", g=2))))

            inp_stage(0)

            for t in range(NT2):
                par = t % 2

                if t + 2 < NT2:
                    psem = LB if (t + 2) % 2 == 1 else LA
                    pw = [(TE, TE.n), (VE, VE.n)]
                    s0 = order[t + 2] * T2
                    for dst, srcn, cd in [(zt[par], "z_sp", 8), (qgt[par], "qg_sp", 8),
                                          (ot[par], "o_sp", 16)]:
                        stage(tq, "sync", pw, [(psem, 16)],
                              lambda s, att, dst=dst, srcn=srcn, cd=cd, s0=s0: att(
                                  s.dma_start(out=dst[:, :].rearrange("p (c n) -> p c n", c=cd),
                                              in_=io[srcn][:, s0:s0 + T2].rearrange("(c p) n -> p c n", p=128))))
                        pw = []
                    in_ready[t + 2] = (psem, psem.n)

                def osqb(scalar, att, par=par):
                    scalar.drain()
                    att(scalar.activation(osq[:, :], ofl[par][:, :], ACT.Square))
                stage(tq, "scalar", [(VE, cev3_arr[t])], [(SE, 1)], osqb)
                osq_done_arr[t] = SE.n

                def oflg(vector, att, par=par):
                    vector.drain()
                    att(vector.tensor_tensor(ogb[par][:, :], ofl[par][:, :],
                                             sg[par][:, :], ALU.mult))
                ow = [(SE, osq_done_arr[t])]
                if t >= 2:
                    ow.append((TE, wo1_arr[t - 2]))
                stage(tq, "vector", ow, [(VE, 1)], oflg)

                if t + 1 < NT2:
                    inp_stage(t + 1)

                def onesmm(tensor, att):
                    for h in range(4):
                        for ec in range(4):
                            instr = tensor.matmul(
                                ps[0:1, h * T2:(h + 1) * T2],
                                P["onesb"][:, :],
                                osq[:, (h * 4 + ec) * T2:(h * 4 + ec + 1) * T2],
                                start=(ec == 0), stop=(ec == 3))
                    att(instr)
                stage(tq, "tensor", [(SE, osq_done_arr[t])] + gw("ps"), [(TE, 1)], onesmm)
                ones_after_arr[t] = TE.n

                def srowb(vector, att, par=par):
                    vector.drain()
                    att(vector.tensor_scalar(srow[par][:, :], ps[0:1, :], 1.0 / DV, EPS,
                                             ALU.mult, ALU.add))
                sw = [(TE, TE.n)]
                if t == 2 and ubsnd_holder[0] is not None:
                    sw.append((GP, ubsnd_holder[0]))
                stage(tq, "vector", sw, [(VE, 1)], srowb)
                gset("ps", VE)

                def rsq(scalar, att, par=par):
                    scalar.drain()
                    scalar.activation(srow[par][:, :], srow[par][:, :], ACT.Ln)
                    scalar.drain()
                    att(scalar.activation(srow[par][:, :], srow[par][:, :], ACT.Exp,
                                          scale=-0.5))
                stage(tq, "scalar", [(VE, VE.n)], [(SE, 1)], rsq)
                inv_done_arr[t] = SE.n

                if t > 0:
                    apply_stage(t - 1)
                    spill_done = GP.n
                    if t == 1:
                        stage(tq, "gpsimd", [(VE, VE.n)], [(GP, 16)],
                              lambda g, att: att(g.dma_start(out=ubrow[:, :],
                                                             in_=ut[127:128, D:2 * D])))
                        spill_done = GP.n

                        def ubm(vector, att):
                            vector.drain()
                            att(vector.tensor_scalar(ubrow[:, :], ubrow[:, :],
                                                     P["role"][0:1, 0:1], None,
                                                     ALU.mult, ALU.bypass))
                        stage(tq, "vector", [(GP, GP.n)], [(VE, 1)], ubm)

                        stage(tq, "gpsimd", [(VE, VE.n)], [(GP, 16)],
                              lambda g, att: att(g.dma_start(out=io["ub_snd"][:, :],
                                                             in_=ubrow[:, :])))
                        ubsnd_val = GP.n
                        ubsnd_holder[0] = GP.n
                        spill_done = GP.n
                        if not dev:
                            stage(tq, "gpsimd", [(GP, ubsnd_val)], [(CC, 1)],
                                  lambda g, att: att(g.collective_compute(
                                      "AllReduce", ALU.add, replica_groups=PAIRS,
                                      ins=[io["ub_snd"].opt()], outs=[io["ub_rcv"].opt()])))
                    s0 = order[t] * T2
                    stage(tq, "sync", [(VE, ub_done_arr[t - 1])], [(LD, 16)],
                          lambda s, att, s0=s0: att(
                              s.dma_start(out=yt[:, :].rearrange("p (g n) -> p g n", g=2),
                                          in_=io["y_sp"][s0:s0 + T2, :].rearrange("(g p) n -> p g n", p=128))))
                    y_ready[t] = (LD, LD.n)
                tq.flush()

            apply_stage(NT2 - 1)
            tq.flush()

            @blk.gpsimd
            def _(gpsimd):
                wt(gpsimd, VE)
                GP.inc(gpsimd.dma_start(out=io["l3_sp"][:, 0:32], in_=P["l3m"][:, :]), 16)
                GP.inc(gpsimd.dma_start(out=io["l3_sp"][:, 32:64], in_=P["l3i"][:, :]), 16)
                wt(gpsimd, GP)


def make_consts():
    consts = {}
    consts["identb"] = np.eye(128, dtype=np.float32).astype(ml_dtypes.bfloat16)
    caus = np.zeros((128, 2 * C), dtype=np.float32)
    for sg_ in range(2):
        for p in range(128):
            caus[p, sg_ * C + sg_ * 128 + p:(sg_ + 1) * C] = 1.0
    consts["caus"] = caus.astype(ml_dtypes.bfloat16)
    return consts


def make_in_maps(inputs):
    x = np.ascontiguousarray(inputs["L0_attn_x"], dtype=np.float32)
    w = {
        "wq": inputs["L0_attn_Wq"], "wk": inputs["L0_attn_Wk"],
        "wv": inputs["L0_attn_Wv"], "wg": inputs["L0_attn_Wg"],
        "gw1": inputs["L0_attn_gk_w1"], "gw2": inputs["L0_attn_gk_w2"],
        "gb2": np.reshape(inputs["L0_attn_gk_b2"], (1, D)),
        "gnw": np.reshape(inputs["L0_attn_g_norm_w"], (1, DV)),
        "wo": inputs["L0_attn_Wo"],
        "fr": inputs["L0_ffn_W_r"], "fk": inputs["L0_ffn_W_k"],
        "fv": inputs["L0_ffn_W_v"],
        "mur": np.reshape(inputs["L0_ffn_mu_r"], (1, D)),
        "muk": np.reshape(inputs["L0_ffn_mu_k"], (1, D)),
    }
    w = {k: np.ascontiguousarray(v, dtype=np.float32) for k, v in w.items()}
    w.update(make_consts())

    in_maps = []
    for core in range(8):
        b, half = core // 2, core % 2
        t0 = half * SEG
        pad = np.zeros((128, D), dtype=np.float32)
        if half == 1:
            pad[127] = x[b, t0 - 1]
        m = dict(w)
        m["x"] = np.concatenate([pad, x[b, t0:t0 + SEG]], axis=0)
        m["role"] = np.full((128, 1), 1.0 if half == 0 else 0.0, np.float32)
        m["use"] = np.full((128, 1), 1.0 if half == 1 else 0.0, np.float32)
        in_maps.append(m)
    return in_maps


def kernel(**inputs):
    nc = build_program()
    in_maps = make_in_maps(inputs)
    res = run_bass_kernel_spmd(nc, in_maps, core_ids=list(range(8)))
    out = np.empty((4, 8192, D), dtype=np.float32)
    for core in range(8):
        b, half = core // 2, core % 2
        out[b, half * SEG:(half + 1) * SEG] = res.results[core]["out"]
    return out
